# version 03 (final)
# baseline (speedup 1.0000x reference)
import numpy as np
import ml_dtypes
import concourse.bass as bass
import concourse.mybir as mybir
from concourse.bass_utils import run_bass_kernel_spmd
from concourse.bass_core_rust import IndirectOffsetOnAxis

F32 = mybir.dt.float32
BF16 = mybir.dt.bfloat16
I32 = mybir.dt.int32
I16 = mybir.dt.int16
U32 = mybir.dt.uint32
ALU = mybir.AluOpType
AF = mybir.ActivationFunctionType
AX = mybir.AxisListType

SEQ = 2048
DM = 1024
NCORE = 8
SEQ_PER_CORE = 2
NCH = DM // 128

BF = ml_dtypes.bfloat16

ENGINES = ("sync", "vector", "scalar", "tensor", "gpsimd")


class Buf:
    __slots__ = ("h", "name", "writer", "readers")

    def __init__(self, h, name):
        self.h = h
        self.name = name
        self.writer = None
        self.readers = []

    def ap(self, *idx):
        a = self.h.ap()
        return a.__getitem__(idx) if idx else a

    def __getitem__(self, idx):
        return self.h.ap().__getitem__(idx)


class P:

    def __init__(self, name="prog", scratch=16384):
        self.nc = bass.Bass("TRN2", target_bir_lowering=False,
                            dynamic_dma_scratch_size=scratch)
        self.ops = {e: [] for e in ENGINES}
        self.cnt = {e: 0 for e in ENGINES}
        self.waited = {}
        self.sems = {e: self.nc.alloc_semaphore(f"sem_{e}") for e in ENGINES}
        self._scratch = {}
        self._const_cache = {}
        self.const_inputs = {}
        self.PB = [Buf(self.nc.alloc_psum_tensor(f"pb{i}", [128, 512], F32),
                       f"pb{i}") for i in range(4)]
        self.PBW = Buf(self.nc.alloc_psum_tensor("pbw", [128, 1024], F32),
                       "pbw")
        self.PBbf = Buf(self.nc.alloc_psum_tensor("pbbf", [128, 512], BF16),
                        "pbbf")
        self.SCR = [Buf(self.nc.alloc_sbuf_tensor(f"scr{i}", [128, 512], F32),
                        f"scr{i}") for i in range(3)]

    def sb(self, name, shape, dtype):
        key = ("sb", name)
        if key in self._scratch:
            b = self._scratch[key]
            assert list(b.h.shape) == list(shape), (name, b.h.shape, shape)
            return b
        b = Buf(self.nc.alloc_sbuf_tensor(name, shape, dtype), name)
        self._scratch[key] = b
        return b

    def dram(self, name, shape, dtype, kind="Internal"):
        key = ("dr", name)
        if key in self._scratch:
            return self._scratch[key]
        b = Buf(self.nc.dram_tensor(name, shape, dtype, kind=kind), name)
        self._scratch[key] = b
        return b

    def fconst(self, val):
        key = ("fc", float(val))
        if key in self._scratch:
            return self._scratch[key]
        b = self.sb(f"fc_{len(self._scratch)}", [128, 1], F32)
        self.memset(b, b.ap(), float(val))
        self._scratch[key] = b
        return b

    def const(self, name, arr):
        if name in self._const_cache:
            return self._const_cache[name]
        arr = np.ascontiguousarray(arr)
        dt = {np.dtype(np.float32): F32, np.dtype(BF): BF16,
              np.dtype(np.int32): I32, np.dtype(np.int16): I16,
              np.dtype(np.uint32): U32}[arr.dtype]
        b = self.dram(name, list(arr.shape), dt, kind="ExternalInput")
        self.const_inputs[name] = arr
        self._const_cache[name] = b
        return b

    def _wait(self, eng, tok):
        src, val = tok
        if src == eng and eng == "tensor":
            return
        key = (eng, src)
        if self.waited.get(key, 0) >= val:
            return
        self.waited[key] = val
        sem = self.sems[src]
        self.ops[eng].append(lambda E, sem=sem, val=val: E.wait_ge(sem, val))

    def emit(self, eng, fn, outs=(), ins=(), inc=1):
        deps = []
        for b in ins:
            if b.writer is not None:
                deps.append(b.writer)
        for b in outs:
            if b.writer is not None:
                deps.append(b.writer)
            deps.extend(b.readers)
        best = {}
        for (src, val) in deps:
            if best.get(src, -1) < val:
                best[src] = val
        for src, val in sorted(best.items()):
            self._wait(eng, (src, val))
        self.cnt[eng] += inc
        tok = (eng, self.cnt[eng])
        sem = self.sems[eng]

        def closure(E, fn=fn, sem=sem, inc=inc):
            fn(E).then_inc(sem, inc)

        self.ops[eng].append(closure)
        for b in ins:
            b.readers.append(tok)
        for b in outs:
            b.writer = tok
            b.readers = []
        return tok

    def finalize_timing(self, outdma_bufs):
        for b in outdma_bufs:
            if b.writer is not None:
                self._wait("gpsimd", b.writer)

    def build(self):
        with self.nc.Block() as block:
            def run(E, eng):
                for cl in self.ops[eng]:
                    cl(E)
            block.sync(lambda E: run(E, "sync"))
            block.vector(lambda E: run(E, "vector"))
            block.scalar(lambda E: run(E, "scalar"))
            block.tensor(lambda E: run(E, "tensor"))
            block.gpsimd(lambda E: run(E, "gpsimd"))
        return self.nc

    def dma(self, dst, dst_ap, src, src_ap, eng="sync"):
        return self.emit(eng, lambda E: E.dma_start(out=dst_ap, in_=src_ap),
                         outs=[dst], ins=[src], inc=16)

    def mm(self, out, out_ap, lhsT, lhsT_ap, rhs, rhs_ap, start=True,
           stop=True, tile_position=None):
        return self.emit("tensor",
                         lambda E: E.matmul(out_ap, lhsT_ap, rhs_ap,
                                            start=start, stop=stop,
                                            tile_position=tile_position),
                         outs=[out], ins=[lhsT, rhs])

    def transp(self, out, out_ap, in_, in_ap, iden, iden_ap=None):
        if iden_ap is None:
            iden_ap = iden.h.ap()
        return self.emit("tensor",
                         lambda E: E.transpose(out_ap, in_ap, iden_ap),
                         outs=[out], ins=[in_, iden])

    def act(self, out, out_ap, in_, in_ap, func, bias=0.0, scale=1.0,
            bias_buf=None):
        ins = [in_] + ([bias_buf] if bias_buf is not None else [])
        if isinstance(bias, float) and bias != 0.0:
            bb = self.fconst(bias)
            ins.append(bb)
            bias = bb.ap()
        if isinstance(scale, float) and scale != 1.0:
            sb_ = self.fconst(scale)
            ins.append(sb_)
            scale = sb_.ap()
        return self.emit("scalar",
                         lambda E: E.activation(out_ap, in_ap, func,
                                                bias=bias, scale=scale),
                         outs=[out], ins=ins)

    def tt(self, out, out_ap, a, a_ap, b, b_ap, op, eng="vector"):
        return self.emit(eng,
                         lambda E: E.tensor_tensor(out_ap, a_ap, b_ap, op),
                         outs=[out], ins=[a, b])

    def ts(self, out, out_ap, in_, in_ap, scalar, op, eng="vector",
           scalar_buf=None):
        ins = [in_] + ([scalar_buf] if scalar_buf is not None else [])
        return self.emit(eng,
                         lambda E: E.tensor_single_scalar(out_ap, in_ap,
                                                          scalar, op),
                         outs=[out], ins=ins)

    def stt(self, out, out_ap, in0, in0_ap, scalar, in1, in1_ap, op0, op1,
            eng="vector", scalar_buf=None):
        ins = [in0, in1] + ([scalar_buf] if scalar_buf is not None else [])
        return self.emit(eng,
                         lambda E: E.scalar_tensor_tensor(out_ap, in0_ap,
                                                          scalar, in1_ap,
                                                          op0, op1),
                         outs=[out], ins=ins)

    def copy(self, out, out_ap, in_, in_ap, eng="vector"):
        return self.emit(eng, lambda E: E.tensor_copy(out_ap, in_ap),
                         outs=[out], ins=[in_])

    def memset(self, out, out_ap, val, eng="vector"):
        return self.emit(eng, lambda E: E.memset(out_ap, val), outs=[out])

    def red(self, out, out_ap, in_, in_ap, op, negate=None):
        return self.emit("vector",
                         lambda E: E.tensor_reduce(out_ap, in_ap, AX.X, op,
                                                   negate=negate),
                         outs=[out], ins=[in_])

    def recip(self, out, out_ap, in_, in_ap):
        return self.emit("vector", lambda E: E.reciprocal(out_ap, in_ap),
                         outs=[out], ins=[in_])

    def scan(self, out, out_ap, d0, d0_ap, d1, d1_ap, init, op0, op1):
        return self.emit("vector",
                         lambda E: E.tensor_tensor_scan(out_ap, d0_ap, d1_ap,
                                                        init, op0, op1),
                         outs=[out], ins=[d0, d1])

    def max8(self, out, out_ap, in_, in_ap):
        return self.emit("vector", lambda E: E.max(out=out_ap, in_=in_ap),
                         outs=[out], ins=[in_])

    def max_index(self, out, out_ap, mx, mx_ap, in_, in_ap):
        return self.emit("vector",
                         lambda E: E.max_index(out_ap, mx_ap, in_ap),
                         outs=[out], ins=[mx, in_])

    def match_replace(self, out, out_ap, mx, mx_ap, in_, in_ap, imm):
        return self.emit("vector",
                         lambda E: E.match_replace(out=out_ap,
                                                   in_to_replace=mx_ap,
                                                   in_values=in_ap,
                                                   imm_value=imm),
                         outs=[out], ins=[mx, in_])

    def igather(self, dst, dst_ap, src, src_ap, idx, idx_ap, bound,
                compute_op=ALU.bypass):
        return self.emit("gpsimd",
                         lambda E: E.indirect_dma_start(
                             dst_ap, None, src_ap,
                             IndirectOffsetOnAxis(idx_ap, 0),
                             bounds_check=bound, oob_is_err=False,
                             compute_op=compute_op),
                         outs=[dst], ins=[src, idx], inc=16)

    def iscatter(self, dst, dst_ap, src, src_ap, idx, idx_ap, bound):
        return self.emit("gpsimd",
                         lambda E: E.indirect_dma_start(
                             dst_ap, IndirectOffsetOnAxis(idx_ap, 0),
                             src_ap, None,
                             bounds_check=bound, oob_is_err=False),
                         outs=[dst], ins=[src, idx], inc=16)

    def allgather(self, out, in_):
        return self.emit("gpsimd",
                         lambda E: E.collective_compute(
                             "AllGather", ALU.bypass, [list(range(NCORE))],
                             [in_.h.ap().opt()], [out.h.ap().opt()]),
                         outs=[out], ins=[in_], inc=1)



def sbconst(p, name, arr, eng="sync"):
    key = ("sbc", name)
    if key in p._scratch:
        return p._scratch[key]
    hbm = p.const(name + "_h", arr)
    b = p.sb(name, list(arr.shape), {np.dtype(BF): BF16,
                                     np.dtype(np.float32): F32,
                                     np.dtype(np.int32): I32,
                                     np.dtype(np.int16): I16,
                                     np.dtype(np.uint32): U32}[arr.dtype])
    p.dma(b, b.ap(), hbm, hbm.ap(), eng=eng)
    p._scratch[key] = b
    return b


def std_consts(p):
    iden = sbconst(p, "c_iden", np.eye(128, dtype=BF))
    ones_bf = sbconst(p, "c_ones_bf", np.ones((128, 128), dtype=BF))
    ones_f32 = sbconst(p, "c_ones_f32", np.ones((128, 128), dtype=np.float32))
    return {"iden": iden, "ones_bf": ones_bf, "ones_f32": ones_f32}


def emit_postln(p, stream, fout, C, eps=1e-5, T=SEQ, tag=""):
    S1v = p.PBW[:, 0:512]
    S2v = p.PBW[:, 512:1024]
    S1 = S2 = p.PBW
    ysq = p.SCR[0]
    A = p.SCR[1]
    Bb = p.SCR[2]
    ones_bf = C["ones_bf"]
    ones_f32 = C["ones_f32"]
    ntile = T // 512
    for t in range(ntile):
        for c in range(NCH):
            csl = slice(c * T + t * 512, c * T + t * 512 + 512)
            p.tt(stream, stream[:, csl], stream, stream[:, csl],
                 fout, fout[:, csl], ALU.add)
            p.act(ysq, ysq.ap(), stream, stream[:, csl], AF.Square)
            p.mm(S1, S1v, ones_f32, ones_f32.ap(), stream,
                 stream[:, csl], start=(c == 0), stop=(c == NCH - 1))
            p.mm(S2, S2v, ones_f32, ones_f32.ap(), ysq, ysq.ap(),
                 start=(c == 0), stop=(c == NCH - 1))
        p.ts(A, A.ap(), S1, S1v, 1.0 / DM, ALU.mult)
        p.ts(Bb, Bb.ap(), S2, S2v, 1.0 / DM, ALU.mult)
        p.act(ysq, ysq.ap(), A, A.ap(), AF.Square)
        p.tt(Bb, Bb.ap(), Bb, Bb.ap(), ysq, ysq.ap(), ALU.subtract)
        p.act(Bb, Bb.ap(), Bb, Bb.ap(), AF.Sqrt, bias=eps)
        p.recip(Bb, Bb.ap(), Bb, Bb.ap())
        p.tt(A, A.ap(), A, A.ap(), Bb, Bb.ap(), ALU.mult)
        for c in range(NCH):
            csl = slice(c * T + t * 512, c * T + t * 512 + 512)
            p.tt(stream, stream[:, csl], stream, stream[:, csl],
                 Bb, Bb.ap(), ALU.mult)
            p.tt(stream, stream[:, csl], stream, stream[:, csl],
                 A, A.ap(), ALU.subtract)


def load_weight_lhsT(p, name, w, transpose, C, scale=1.0):
    hbm = p.const(name, np.ascontiguousarray(w, dtype=np.float32))
    if not transpose:
        K, M = w.shape
        buf = p.sb(name + "_sb", [128, (K // 128) * M], BF16)
        if scale == 1.0:
            for c in range(K // 128):
                p.dma(buf, buf[:, c * M:(c + 1) * M],
                      hbm, hbm[c * 128:(c + 1) * 128, :], eng="gpsimd")
        else:
            f32 = p.sb(name + "_f32", [128, M], F32)
            for c in range(K // 128):
                p.dma(f32, f32.ap(), hbm, hbm[c * 128:(c + 1) * 128, :])
                p.ts(buf, buf[:, c * M:(c + 1) * M], f32, f32.ap(),
                     scale, ALU.mult)
        return buf, K, M
    else:
        M, K = w.shape
        buf = p.sb(name + "_sb", [128, (K // 128) * M], BF16)
        tmp = p.sb(name + "_tmp", [128, K], BF16)
        f32 = p.sb(name + "_tf32", [128, K], F32) if scale != 1.0 else None
        for mc in range(M // 128):
            if scale == 1.0:
                p.dma(tmp, tmp.ap(), hbm, hbm[mc * 128:(mc + 1) * 128, :],
                      eng="gpsimd")
            else:
                p.dma(f32, f32.ap(), hbm, hbm[mc * 128:(mc + 1) * 128, :])
                p.ts(tmp, tmp.ap(), f32, f32.ap(), scale, ALU.mult)
            for kc in range(K // 128):
                p.transp(p.PBbf, p.PBbf[:, 0:128], tmp,
                         tmp[:, kc * 128:(kc + 1) * 128], C["iden"])
                p.copy(buf, buf[:, kc * M + mc * 128: kc * M + (mc + 1) * 128],
                       p.PBbf, p.PBbf[:, 0:128])
        return buf, K, M


def wtile(buf, M, kc, mlo, mhi):
    return buf[:, kc * M + mlo: kc * M + mhi]


def load_weight_lhsT_slice(p, name, hbm, bufname, mlo, mhi, C, scale=1.0,
                           tmp=None, f32=None, rowperm=None):
    M, K = hbm.h.shape
    msz = mhi - mlo
    buf = p.sb(bufname, [128, (K // 128) * msz], BF16)
    if tmp is None:
        tmp = p.sb(bufname + "_t", [128, K], BF16)
    for mc in range(msz // 128):
        row = mlo + mc * 128
        if rowperm is None:
            p.dma(tmp, tmp[:, 0:K], hbm, hbm[row:row + 128, :], eng="gpsimd")
        else:
            for hh in range(2):
                for oo in range(2):
                    j0 = hh * 64 + oo * 32
                    srcap = hbm[row + hh * 64 + oo:row + (hh + 1) * 64:2, :]
                    p.dma(tmp, tmp[j0:j0 + 32, 0:K], hbm, srcap,
                          eng="gpsimd")
        for kc in range(K // 128):
            p.transp(p.PBbf, p.PBbf[:, 0:128], tmp,
                     tmp[:, kc * 128:(kc + 1) * 128], C["iden"])
            dst = buf[:, kc * msz + mc * 128: kc * msz + (mc + 1) * 128]
            if scale == 1.0:
                p.copy(buf, dst, p.PBbf, p.PBbf[:, 0:128])
            else:
                p.ts(buf, dst, p.PBbf, p.PBbf[:, 0:128], scale, ALU.mult)
    return buf


def body_l1_ffn(p, stream, fout, W, C, T=SEQ):
    Q = 1.0 / 127.0
    HID = 3072
    wg_h = p.const("l1f_wg", W["L1_ffn_w_gate"].astype(np.int32))
    wu_h = p.const("l1f_wu", W["L1_ffn_w_up"].astype(np.int32))
    wd_h = p.const("l1f_wd", W["L1_ffn_w_down"].astype(np.int32))
    stg_t = p.sb("l1f_stgt", [128, 3072], BF16)
    wd = load_weight_lhsT_slice(p, "l1f_wd", wd_h, "l1f_wds", 0, 1024, C,
                                scale=Q, tmp=stg_t)
    hid = p.sb("l1f_hid", [128, 24 * 512], BF16)
    sbf = p.sb("l1f_sbf", [128, NCH * 512], BF16)
    tmp = p.sb("l1f_tmp", [128, 512], F32)
    for th in range(4):
        t0 = th * 512
        for c in range(NCH):
            p.copy(sbf, sbf[:, c * 512:(c + 1) * 512], stream,
                   stream[:, c * T + t0:c * T + t0 + 512])
        for hs in range(HID // 512):
            wg = load_weight_lhsT_slice(p, "l1f_wg", wg_h, "l1f_wgs",
                                        hs * 512, hs * 512 + 512, C, scale=Q,
                                        tmp=stg_t)
            wu = load_weight_lhsT_slice(p, "l1f_wu", wu_h, "l1f_wus",
                                        hs * 512, hs * 512 + 512, C, scale=Q,
                                        tmp=stg_t)
            for hq in range(4):
                mlo = hq * 128
                g = p.PB[0]
                u = p.PB[1]
                for c in range(NCH):
                    rhs = sbf[:, c * 512:(c + 1) * 512]
                    p.mm(g, g.ap(), wg, wtile(wg, 512, c, mlo, mlo + 128),
                         sbf, rhs, start=(c == 0), stop=(c == NCH - 1))
                    p.mm(u, u.ap(), wu, wtile(wu, 512, c, mlo, mlo + 128),
                         sbf, rhs, start=(c == 0), stop=(c == NCH - 1))
                p.act(tmp, tmp.ap(), g, g.ap(), AF.Silu)
                hc = hs * 4 + hq
                p.tt(hid, hid[:, hc * 512:(hc + 1) * 512],
                     tmp, tmp.ap(), u, u.ap(), ALU.mult)
        for oc in range(NCH):
            pb = p.PB[oc % 4]
            for hc in range(24):
                p.mm(pb, pb.ap(), wd,
                     wtile(wd, 1024, hc, oc * 128, (oc + 1) * 128),
                     hid, hid[:, hc * 512:(hc + 1) * 512],
                     start=(hc == 0), stop=(hc == 23))
            p.copy(fout, fout[:, oc * T + t0:oc * T + t0 + 512],
                   pb, pb.ap())


def stream_to_bf(p, stream, T=SEQ):
    sbf = p.sb("stream_bf", [128, NCH * T], BF16)
    for c in range(NCH):
        p.copy(sbf, sbf[:, c * T:(c + 1) * T], stream,
               stream[:, c * T:(c + 1) * T])
    return sbf



def body_l2_ffn(p, stream, fout, W, C, T=SEQ):
    wbuf, K, M = load_weight_lhsT(p, "l2f_w", W["L2_ffn_weight"], True, C)
    dbuf = p.const("l2f_disha",
                   np.ascontiguousarray(W["L2_ffn_disha"], dtype=np.float32))
    dsb = p.sb("l2f_dsb", [64, 1024], BF16)
    p.dma(dsb, dsb.ap(), dbuf, dbuf.ap(), eng="gpsimd")
    sbf = stream_to_bf(p, stream, T)
    p64 = np.zeros((128, 64), dtype=BF)
    for i in range(128):
        p64[i, i % 64] = 1
    P64 = sbconst(p, "c_p64", p64)
    pooled = p.sb("l2f_pooled", [64, T], BF16)
    for t in range(T // 512):
        tsl = slice(t * 512, (t + 1) * 512)
        pb = p.PB[0]
        for c in range(NCH):
            p.mm(pb, pb[0:64, :], P64, P64.ap(), sbf,
                 sbf[:, c * T + t * 512:c * T + t * 512 + 512],
                 start=(c == 0), stop=(c == NCH - 1))
        p.copy(pooled, pooled[:, tsl], pb, pb[0:64, :])
    for oc in range(NCH):
        for t in range(T // 512):
            pb = p.PB[1]
            for c in range(NCH):
                p.mm(pb, pb.ap(), wbuf, wtile(wbuf, M, c, oc * 128,
                                              (oc + 1) * 128),
                     sbf, sbf[:, c * T + t * 512:c * T + t * 512 + 512],
                     start=(c == 0), stop=False)
            p.mm(pb, pb.ap(), dsb, dsb[:, oc * 128:(oc + 1) * 128],
                 pooled, pooled[:, t * 512:(t + 1) * 512],
                 start=False, stop=True)
            p.copy(fout, fout[:, oc * T + t * 512:oc * T + t * 512 + 512],
                   pb, pb.ap())



def body_l2_attn(p, stream, fout, W, C, T=SEQ):
    H = 8
    wqkv_h = p.const("l2a_wqkv", W["L2_attn_w_qkv"].astype(np.float32))
    wout_h = p.const("l2a_wout", W["L2_attn_w_out"].astype(np.float32))
    xbf = fout
    for c in range(NCH):
        p.copy(xbf, xbf[:, c * T:(c + 1) * T], stream,
               stream[:, c * T:(c + 1) * T])
    qT = p.sb("l2a_qT", [128, H * T + 16], BF16)
    kT = p.sb("l2a_kT", [128, H * T + 16], BF16)
    wpart = p.sb("l2a_wpart", [128, NCH * 1024], BF16)
    vtok = p.dram("l2a_vtok", [T + 16, 1024], BF16)
    voutE = p.dram("l2a_voutE", [1024, T], BF16)
    voutO = p.dram("l2a_voutO", [1024, T], BF16)
    ztile = p.sb("l2a_z", [128, 1024], BF16)
    p.memset(ztile, ztile.ap(), 0.0)
    p.dma(vtok, vtok[T:T + 16, :], ztile, ztile[0:16, :], eng="gpsimd")
    for cc in range(NCH):
        p.dma(voutO, voutO[cc * 128:(cc + 1) * 128, 0:2], ztile,
              ztile[:, 0:2], eng="gpsimd")
        p.dma(voutO, voutO[cc * 128:(cc + 1) * 128, T - 2:T], ztile,
              ztile[:, 0:2], eng="gpsimd")

    for part, dst in ((0, qT), (1, kT)):
        for c in range(NCH):
            p.dma(wpart, wpart[:, c * 1024:(c + 1) * 1024], wqkv_h,
                  wqkv_h[c * 128:(c + 1) * 128,
                         part * 1024:(part + 1) * 1024], eng="gpsimd")
        for oc in range(H):
            for tq in range(T // 512):
                pb = p.PB[tq % 2]
                for c in range(NCH):
                    p.mm(pb, pb.ap(), wpart,
                         wpart[:, c * 1024 + oc * 128:c * 1024 + (oc + 1) * 128],
                         xbf, xbf[:, c * T + tq * 512:c * T + tq * 512 + 512],
                         start=(c == 0), stop=(c == NCH - 1))
                p.copy(dst, dst[:, oc * T + tq * 512:oc * T + tq * 512 + 512],
                       pb, pb.ap())
        p.memset(dst, dst[:, H * T:H * T + 16], 0.0)
    for c in range(NCH):
        p.dma(wpart, wpart[:, c * 1024:(c + 1) * 1024], wqkv_h,
              wqkv_h[c * 128:(c + 1) * 128, 2048:3072], eng="gpsimd")
    vsb = p.sb("l2a_vsb", [128, 1024], BF16)
    for tt in range(T // 128):
        for half in range(2):
            for c in range(NCH):
                p.mm(p.PBW, p.PBW[:, half * 512:(half + 1) * 512], xbf,
                     xbf[:, c * T + tt * 128:c * T + (tt + 1) * 128],
                     wpart, wpart[:, c * 1024 + half * 512:
                                  c * 1024 + (half + 1) * 512],
                     start=(c == 0), stop=(c == NCH - 1))
        p.copy(vsb, vsb.ap(), p.PBW, p.PBW.ap())
        p.dma(vtok, vtok[tt * 128:(tt + 1) * 128, :], vsb, vsb.ap(),
              eng="gpsimd")

    blk = np.full((128, 128), -1e9, dtype=np.float32)
    for w in range(32):
        blk[w * 4:(w + 1) * 4, w * 4:(w + 1) * 4] = 0.0
    mask_std = sbconst(p, "l2a_mstd", blk)
    blk15 = blk.copy()
    blk15[124:, :] = -1e9
    blk15[:, 124:] = -1e9
    mask_o15 = sbconst(p, "l2a_mo15", blk15)
    vtile = p.sb("l2a_vtile", [128, 1024], BF16)
    EE = p.sb("l2a_EE", [128, 128], BF16)
    PVsb = p.sb("l2a_PVsb", [128, 128], BF16)
    rden = p.sb("l2a_rden", [1, 128], BF16)
    rdf = p.sb("l2a_rdf", [1, 128], F32)
    vo = p.sb("l2a_vo", [128, H * 128], BF16)
    for tile in range(16):
        for pi, (off, vout_d) in enumerate(((0, voutE), (2, voutO))):
            t0 = tile * 128 + off
            mask = mask_o15 if (pi == 1 and tile == 15) else mask_std
            p.dma(vtile, vtile.ap(), vtok, vtok[t0:t0 + 128, :], eng="sync")
            for h in range(H):
                S = p.PB[0]
                PV = p.PB[1]
                DEN = p.PB[2]
                BC = p.PB[3]
                p.mm(S, S[:, 0:128], kT, kT[:, h * T + t0:h * T + t0 + 128],
                     qT, qT[:, h * T + t0:h * T + t0 + 128])
                p.tt(S, S[:, 0:128], S, S[:, 0:128], mask, mask.ap(), ALU.add)
                p.act(EE, EE.ap(), S, S[:, 0:128], AF.Exp, scale=1.0 / 32.0)
                p.mm(PV, PV[:, 0:128], vtile,
                     vtile[:, h * 128:(h + 1) * 128], EE, EE.ap())
                p.mm(DEN, DEN[0:1, 0:128], C["ones_bf"],
                     C["ones_bf"].h.ap()[:, 0:1], EE, EE.ap())
                p.recip(rdf, rdf.ap(), DEN, DEN[0:1, 0:128])
                p.copy(rden, rden.ap(), rdf, rdf.ap())
                p.mm(BC, BC[:, 0:128], C["ones_bf"],
                     C["ones_bf"].h.ap()[0:1, :], rden, rden.ap())
                p.copy(PVsb, PVsb.ap(), PV, PV[:, 0:128])
                p.tt(vo, vo[:, h * 128:(h + 1) * 128], PVsb, PVsb.ap(),
                     BC, BC[:, 0:128], ALU.mult)
            if pi == 1 and tile == 15:
                dst = vout_d.h.ap()[:, t0:t0 + 124]
                dst = dst.rearrange("(h p) q -> p h q", p=128)
                p.emit("gpsimd", lambda E, dst=dst: E.dma_start(
                    out=dst, in_=vo.h.ap().rearrange(
                        "p (h q) -> p h q", h=H)[:, :, 0:124]),
                    outs=[vout_d], ins=[vo], inc=16)
            else:
                dst = vout_d.h.ap()[:, t0:t0 + 128]
                dst = dst.rearrange("(h p) q -> p h q", p=128)
                p.emit("gpsimd", lambda E, dst=dst: E.dma_start(
                    out=dst, in_=vo.h.ap().rearrange("p (h q) -> p h q", h=H)),
                    outs=[vout_d], ins=[vo], inc=16)

    bout = W["L2_attn_b_out"].astype(np.float32)
    cov = np.full((1, T), 2.0, dtype=np.float32)
    cov[0, :2] = 1.0
    cov[0, -2:] = 1.0
    bc_h = p.const("l2a_bc", np.concatenate(
        [bout.reshape(1, 1024).astype(BF), cov.astype(BF)], axis=1))
    p.dma(kT, kT[0:1, 0:3072], bc_h, bc_h.ap(), eng="gpsimd")
    bout1p = kT
    covb = kT
    vsum = p.sb("l2a_vsum", [128, T], BF16)
    vEc = p.sb("l2a_vsb2", [128, T], BF16)
    for c in range(NCH):
        p.dma(vEc, vEc[:, 0:T], voutE, voutE[c * 128:(c + 1) * 128, :],
              eng="sync")
        p.dma(vsum, vsum.ap(), voutO, voutO[c * 128:(c + 1) * 128, :],
              eng="sync")
        p.tt(vsum, vsum.ap(), vsum, vsum.ap(), vEc, vEc[:, 0:T], ALU.add)
        p.copy(qT, qT[:, c * T:(c + 1) * T], vsum, vsum.ap())
    wob = wpart
    for c in range(NCH):
        p.dma(wob, wob[:, c * 1024:(c + 1) * 1024], wout_h,
              wout_h[c * 128:(c + 1) * 128, :], eng="gpsimd")
    for oc in range(NCH):
        for tq in range(T // 512):
            pb = p.PB[tq % 2]
            for c in range(NCH):
                p.mm(pb, pb.ap(), wob, wtile(wob, 1024, c, oc * 128,
                                             (oc + 1) * 128),
                     qT, qT[:, c * T + tq * 512:c * T + tq * 512 + 512],
                     start=(c == 0), stop=False)
            p.mm(pb, pb.ap(), bout1p, bout1p[0:1, oc * 128:(oc + 1) * 128],
                 covb, covb[0:1, 1024 + tq * 512:1024 + (tq + 1) * 512],
                 start=False, stop=True)
            p.copy(fout, fout[:, oc * T + tq * 512:oc * T + tq * 512 + 512],
                   pb, pb.ap())



def emit_elu1(p, dstbuf, dst_ap, srcbuf, src_ap, tmpbuf, tmp_ap):
    p.ts(tmpbuf, tmp_ap, srcbuf, src_ap, 0.0, ALU.min)
    p.act(tmpbuf, tmp_ap, tmpbuf, tmp_ap, AF.Exp)
    p.stt(dstbuf, dst_ap, srcbuf, src_ap, 0.0, tmpbuf, tmp_ap,
          ALU.max, ALU.add)


def body_l3_attn(p, stream, fout, W, C, T=SEQ):
    H, DH, M = 16, 64, 256
    wq_h = p.const("l3a_wq", W["L3_attn_Wq"].astype(np.float32))
    wk_h = p.const("l3a_wk", W["L3_attn_Wk"].astype(np.float32))
    wv_h = p.const("l3a_wv", W["L3_attn_Wv"].astype(np.float32))
    wo_h = p.const("l3a_wo", W["L3_attn_Wo"].astype(np.float32))
    biases = np.stack([W["L3_attn_bq"], W["L3_attn_bk"], W["L3_attn_bv"],
                       W["L3_attn_bo"]], 0).astype(np.float32)
    bia = sbconst(p, "l3a_bias",
                  np.ascontiguousarray(biases.reshape(4, 8, 128)
                                       .transpose(2, 0, 1).reshape(128, 32)))
    Rdup = np.concatenate([W["L3_attn_random_matrix"]] * 2, axis=0)
    Rm = sbconst(p, "l3a_R", Rdup.astype(BF))
    xbf = fout
    for c in range(NCH):
        p.copy(xbf, xbf[:, c * T:(c + 1) * T], stream,
               stream[:, c * T:(c + 1) * T])
    qm = p.sb("l3a_qm", [128, 2 * T], BF16)
    qT = p.sb("l3a_qT", [128, NCH * T], BF16)
    kT = p.sb("l3a_kT", [128, NCH * T], BF16)
    vtok = p.dram("l3a_vtok", [T, 1024], BF16)
    vtile = p.sb("l3a_vtile", [128, 64], BF16)
    wbuf = p.sb("l3a_wbuf", [128, NCH * 1024], BF16)
    for wh, dst, bi in ((wq_h, qT, 0), (wk_h, kT, 1)):
        for c in range(NCH):
            p.dma(wbuf, wbuf[:, c * 1024:(c + 1) * 1024], wh,
                  wh[c * 128:(c + 1) * 128, :], eng="gpsimd")
        for oc in range(NCH):
            for tq in range(T // 512):
                pb = p.PB[tq % 2]
                for c in range(NCH):
                    p.mm(pb, pb.ap(), wbuf,
                         wbuf[:, c * 1024 + oc * 128:c * 1024 + (oc + 1) * 128],
                         xbf, xbf[:, c * T + tq * 512:c * T + tq * 512 + 512],
                         start=(c == 0), stop=(c == NCH - 1))
                p.ts(dst, dst[:, oc * T + tq * 512:oc * T + tq * 512 + 512],
                     pb, pb.ap(), bia[:, bi * 8 + oc:bi * 8 + oc + 1],
                     ALU.add, scalar_buf=bia)
    for c in range(NCH):
        p.dma(wbuf, wbuf[:, c * 1024:(c + 1) * 1024], wv_h,
              wv_h[c * 128:(c + 1) * 128, :], eng="gpsimd")
    bvh = p.const("l3a_bvh", W["L3_attn_bv"].reshape(1, 1024).astype(BF))
    bv1p = qm
    p.dma(bv1p, bv1p[0:1, 0:1024], bvh, bvh.ap(), eng="gpsimd")
    onesrow = sbconst(p, "l3a_ones1p", np.ones((1, 128), dtype=BF))
    for tt in range(T // 128):
        for half in range(2):
            pv = p.PBW
            sl = slice(half * 512, (half + 1) * 512)
            for c in range(NCH):
                p.mm(pv, pv[:, sl], xbf,
                     xbf[:, c * T + tt * 128:c * T + (tt + 1) * 128],
                     wbuf, wbuf[:, c * 1024 + half * 512:
                                c * 1024 + (half + 1) * 512],
                     start=(c == 0), stop=False)
            p.mm(pv, pv[:, sl], onesrow, onesrow[0:1, 0:128],
                 bv1p, bv1p[0:1, half * 512:(half + 1) * 512],
                 start=False, stop=True)
        p.copy(qm, qm[:, T:T + 1024], p.PBW, p.PBW.ap())
        p.dma(vtok, vtok[tt * 128:(tt + 1) * 128, :], qm, qm[:, T:T + 1024],
              eng="gpsimd")

    obuf = p.dram("l3a_obuf", [1024, T], BF16)
    kp = p.sb("l3a_kp", [128, (T // 128) * M], BF16)
    kva = p.sb("l3a_kva", [128, 2 * 65], BF16)
    eluT = p.sb("l3a_eluT", [128, 512], F32)
    rden = p.sb("l3a_rden", [1, 512], BF16)
    rdf = p.sb("l3a_rdf", [1, 512], F32)
    osb = p.sb("l3a_osb", [64, 512], BF16)
    CLAMP = 1e-6 * 256.0
    for h in range(H):
        hb = (h % 2) * 64
        hc = h // 2
        for tt in range(T // 128):
            pb = p.PB[0]
            p.mm(pb, pb[:, 0:256], kT,
                 kT[hb:hb + 64, hc * T + tt * 128:hc * T + (tt + 1) * 128],
                 Rm, Rm[hb:hb + 64, :], start=True, stop=True)
            emit_elu1(p, kp, kp[:, tt * M:(tt + 1) * M], pb, pb[:, 0:256],
                      eluT, eluT[:, 0:256])
        for mh in range(2):
            for tq in range(T // 512):
                pb = p.PB[1]
                p.mm(pb, pb.ap(), Rm,
                     Rm[hb:hb + 64, mh * 128:(mh + 1) * 128],
                     qT, qT[hb:hb + 64, hc * T + tq * 512:
                            hc * T + tq * 512 + 512],
                     start=True, stop=True)
                emit_elu1(p, qm, qm[:, mh * T + tq * 512:mh * T + tq * 512 + 512],
                          pb, pb.ap(), eluT, eluT[:, 0:512])
        pb0, pb1, ps0, ps1 = p.PB[2], p.PB[1], p.PB[0], p.PB[3]
        for tt in range(T // 128):
            p.dma(vtile, vtile.ap(), vtok,
                  vtok[tt * 128:(tt + 1) * 128, h * 64:(h + 1) * 64],
                  eng="sync")
            for mh, pb, ps in ((0, pb0, ps0), (1, pb1, ps1)):
                p.mm(pb, pb[:, 0:64], kp,
                     kp[:, tt * M + mh * 128:tt * M + (mh + 1) * 128],
                     vtile, vtile.ap(),
                     start=(tt == 0), stop=(tt == T // 128 - 1))
                p.mm(ps, ps[:, 0:1], kp,
                     kp[:, tt * M + mh * 128:tt * M + (mh + 1) * 128],
                     C["ones_bf"], C["ones_bf"].h.ap()[0:128, 0:1],
                     start=(tt == 0), stop=(tt == T // 128 - 1))
        for mh, pb, ps in ((0, pb0, ps0), (1, pb1, ps1)):
            p.copy(kva, kva[:, mh * 65:mh * 65 + 64], pb, pb[:, 0:64])
            p.copy(kva, kva[:, mh * 65 + 64:mh * 65 + 65], ps, ps[:, 0:1])
        for tq in range(T // 512):
            pb = p.PB[3]
            for mh in range(2):
                p.mm(pb, pb[0:65, :], kva, kva[:, mh * 65:(mh + 1) * 65],
                     qm, qm[:, mh * T + tq * 512:mh * T + tq * 512 + 512],
                     start=(mh == 0), stop=(mh == 1))
            p.ts(rdf, rdf.ap(), pb, pb[64:65, :], CLAMP, ALU.max)
            p.recip(rdf, rdf.ap(), rdf, rdf.ap())
            p.copy(rden, rden.ap(), rdf, rdf.ap())
            p.mm(p.PB[0], p.PB[0][0:64, :], onesrow, onesrow[0:1, 0:64],
                 rden, rden.ap(), start=True, stop=True)
            p.copy(eluT, eluT[0:64, :], p.PB[0], p.PB[0][0:64, :])
            p.tt(osb, osb.ap(), pb, pb[0:64, :], eluT, eluT[0:64, :],
                 ALU.mult)
            dst = obuf.h.ap()[hc * 128 + hb:hc * 128 + hb + 64,
                              tq * 512:(tq + 1) * 512]
            p.dma(obuf, dst, osb, osb.ap(), eng="gpsimd")
    oc_in = kp
    for c in range(NCH):
        p.dma(wbuf, wbuf[:, c * 1024:(c + 1) * 1024], wo_h,
              wo_h[c * 128:(c + 1) * 128, :], eng="gpsimd")
    for tq in range(T // 512):
        for c in range(NCH):
            p.dma(oc_in, oc_in[:, c * 512:(c + 1) * 512], obuf,
                  obuf[c * 128:(c + 1) * 128, tq * 512:(tq + 1) * 512],
                  eng="sync")
        for oc in range(NCH):
            pb = p.PB[oc % 2]
            for c in range(NCH):
                p.mm(pb, pb.ap(), wbuf,
                     wbuf[:, c * 1024 + oc * 128:c * 1024 + (oc + 1) * 128],
                     oc_in, oc_in[:, c * 512:(c + 1) * 512],
                     start=(c == 0), stop=(c == NCH - 1))
            p.ts(fout, fout[:, oc * T + tq * 512:oc * T + tq * 512 + 512],
                 pb, pb.ap(), bia[:, 3 * 8 + oc:3 * 8 + oc + 1],
                 ALU.add, scalar_buf=bia)



def body_l0_attn(p, stream, fout, W, C, T=SEQ):
    H, DK = 16, 64
    ROPEPERM = ("(h i o) k -> h o i k", dict(h=2, i=32, o=2))
    wq_h = p.const("l0a_wq", W["L0_attn_Wq"].astype(np.float32))
    wk_h = p.const("l0a_wk", W["L0_attn_Wk"].astype(np.float32))
    wv_h = p.const("l0a_wv", W["L0_attn_Wv"].astype(np.float32))
    wo_h = p.const("l0a_wo", W["L0_attn_Wo"].astype(np.float32))
    def permvec(b):
        v = b.reshape(8, 2, 32, 2)
        return np.ascontiguousarray(v.transpose(1, 3, 2, 0)
                                    .reshape(128, 8))
    bqk = sbconst(p, "l0a_bqk", np.concatenate(
        [permvec(W["L0_attn_bq"].astype(np.float32)),
         permvec(W["L0_attn_bk"].astype(np.float32)),
         np.ascontiguousarray(W["L0_attn_bo"].astype(np.float32)
                              .reshape(8, 128).T)], axis=1))
    inv = 1.0 / (10000.0 ** (np.arange(0, 64, 2, dtype=np.float64) / 64))
    ang = np.arange(T, dtype=np.float64)[None, :] * inv[:, None]
    cos4 = np.tile(np.cos(ang), (4, 1)).astype(BF)
    sin4 = np.tile(np.sin(ang), (4, 1)).astype(BF)
    sb_ = np.sin(ang)
    ssin = np.concatenate([-sb_, sb_, -sb_, sb_], axis=0).astype(BF)
    cosT = sbconst(p, "l0a_cos", cos4)
    ssinT = sbconst(p, "l0a_ssin", ssin)
    hsel = np.zeros((128, 33), dtype=np.float32)
    hsel[0:64, 0] = 1
    hsel[64:128, 32] = 1
    hselC = sbconst(p, "l0a_hsel", hsel)
    mb = np.zeros((128, 896), dtype=BF)
    for pp in range(128):
        mb[pp, :384 + pp] = -1e9
    maskMB = sbconst(p, "l0a_mb", mb)

    xbf = fout
    for c in range(NCH):
        p.copy(xbf, xbf[:, c * T:(c + 1) * T], stream,
               stream[:, c * T:(c + 1) * T])
    qT = p.sb("l0a_qT", [128, NCH * T], BF16)
    kT = p.sb("l0a_kT", [128, NCH * T], BF16)
    vtok = p.dram("l0a_vtok", [T, 1024], BF16)
    obuf = p.dram("l0a_obuf", [1024, T], BF16)
    wbuf = p.sb("l0a_wbuf", [128, 1024], BF16)
    rt0 = p.sb("l0a_rt0", [128, T], BF16)
    rt1 = p.sb("l0a_rt1", [128, T], BF16)
    stg = rt0
    sq = p.SCR[0]
    qn = p.sb("l0a_qn", [1, 64], F32)
    t11 = p.sb("l0a_t11", [1, 1], F32)
    negC = p.sb("l0a_negC", [1, 16], F32)
    biasC = p.sb("l0a_biasC", [128, 1], F32)

    for wh, dst, bi, qk in ((wq_h, qT, 0, 0), (wk_h, kT, 1, 1)):
        wb = load_weight_lhsT_slice(p, "l0a_w", wh, "l0a_wT", 0, 1024, C,
                                    tmp=stg, rowperm=ROPEPERM)
        for oc in range(NCH):
            for tq in range(T // 512):
                pb = p.PB[tq % 2]
                for c in range(NCH):
                    p.mm(pb, pb.ap(), wb, wtile(wb, 1024, c, oc * 128,
                                                (oc + 1) * 128),
                         xbf, xbf[:, c * T + tq * 512:c * T + tq * 512 + 512],
                         start=(c == 0), stop=(c == NCH - 1))
                p.ts(dst, dst[:, oc * T + tq * 512:oc * T + tq * 512 + 512],
                     pb, pb.ap(), bqk[:, bi * 8 + oc:bi * 8 + oc + 1],
                     ALU.add, scalar_buf=bqk)
        for oc in range(NCH):
            ca = dst[:, oc * T:(oc + 1) * T]
            for blk in range(4):
                swb = blk ^ 1
                p.dma(rt0, rt0[blk * 32:(blk + 1) * 32, :], dst,
                      dst[swb * 32:(swb + 1) * 32, oc * T:(oc + 1) * T],
                      eng="gpsimd")
            p.tt(rt1, rt1.ap(), rt0, rt0.ap(), ssinT, ssinT.h.ap(), ALU.mult)
            p.tt(dst, ca, dst, ca, cosT, cosT.h.ap(), ALU.mult)
            p.tt(dst, ca, dst, ca, rt1, rt1.ap(), ALU.add)
            for tq in range(T // 512):
                p.act(sq, sq.ap(), dst,
                      dst[:, oc * T + tq * 512:oc * T + tq * 512 + 512],
                      AF.Square)
                pbn = p.PB[2]
                p.mm(pbn, pbn[0:33, :], hselC, hselC.ap(), sq, sq.ap())
                for hp, prow in ((0, 0), (1, 32)):
                    p.red(t11, t11.ap(), pbn, pbn[prow:prow + 1, :], ALU.max)
                    col = qk * 32 + oc * 2 + hp
                    if tq == 0:
                        p.copy(qn, qn[0:1, col:col + 1], t11, t11.ap())
                    else:
                        p.tt(qn, qn[0:1, col:col + 1], qn,
                             qn[0:1, col:col + 1], t11, t11.ap(), ALU.max)
    p.act(qn, qn.ap(), qn, qn.ap(), AF.Sqrt)
    p.tt(negC, negC.ap(), qn, qn[0:1, 0:16], qn, qn[0:1, 32:48], ALU.mult)
    p.ts(negC, negC.ap(), negC, negC.ap(), -0.125, ALU.mult)
    negCb = p.sb("l0a_negCb", [1, 16], BF16)
    p.copy(negCb, negCb.ap(), negC, negC.ap())

    wv = load_weight_lhsT_slice(p, "l0a_wv", wv_h, "l0a_wT", 0, 1024, C,
                                tmp=stg)
    bvh = p.const("l0a_bvh", W["L0_attn_bv"].reshape(1, 1024).astype(BF))
    p.dma(stg, stg[0:1, 0:1024], bvh, bvh.ap(), eng="gpsimd")
    onesrow = sbconst(p, "l0a_ones1p", np.ones((1, 128), dtype=BF))
    for tt in range(T // 128):
        for half in range(2):
            pv = p.PBW
            sl = slice(half * 512, (half + 1) * 512)
            for c in range(NCH):
                p.mm(pv, pv[:, sl], xbf,
                     xbf[:, c * T + tt * 128:c * T + (tt + 1) * 128],
                     wv, wtile(wv, 1024, c, half * 512, (half + 1) * 512),
                     start=(c == 0), stop=False)
            p.mm(pv, pv[:, sl], onesrow, onesrow[0:1, 0:128],
                 stg, stg[0:1, half * 512:(half + 1) * 512],
                 start=False, stop=True)
        p.copy(wbuf, wbuf.ap(), p.PBW, p.PBW.ap())
        p.dma(vtok, vtok[tt * 128:(tt + 1) * 128, :], wbuf,
              wbuf.ap(), eng="gpsimd")

    vh = p.sb("l0a_vh", [128, (T // 128) * DK], BF16)
    EE = p.sb("l0a_EE", [128, 512], BF16)
    osb = p.sb("l0a_osb", [64, 512], BF16)
    ebc = rt1
    rdf = sq
    rden = rt0
    for h in range(H):
        hc, hp = h // 2, h % 2
        for quarter in range(4):
            p.dma(vh, vh.h.ap()[:, quarter * 4 * DK:(quarter + 1) * 4 * DK]
                  .rearrange("p (a d) -> p a d", d=DK),
                  vtok, vtok.h.ap()[quarter * 512:(quarter + 1) * 512,
                                    h * 64:(h + 1) * 64]
                  .rearrange("(a p) d -> p a d", p=128), eng="sync")
        p.mm(p.PB[3], p.PB[3][:, 0:1], onesrow, onesrow[0:1, 0:128],
             negCb, negCb[0:1, h:h + 1])
        p.copy(biasC, biasC.ap(), p.PB[3], p.PB[3][:, 0:1])
        for qw in range(T // 512):
            nkc = (qw + 1) * 4
            PV = p.PB[0]
            DEN = p.PB[1]
            for kc in range(nkc):
                S = p.PB[2]
                b0 = hp * 64
                p.mm(S, S.ap(), kT,
                     kT[b0:b0 + 64, hc * T + kc * 128:hc * T + (kc + 1) * 128],
                     qT, qT[b0:b0 + 64, hc * T + qw * 512:hc * T + qw * 512 + 512])
                if kc * 128 >= qw * 512:
                    d = kc - qw * 4
                    p.tt(S, S.ap(), S, S.ap(), maskMB,
                         maskMB.h.ap()[:, 384 - 128 * d:896 - 128 * d],
                         ALU.add)
                p.act(EE, EE.ap(), S, S.ap(), AF.Exp, scale=0.125,
                      bias=biasC.ap(), bias_buf=biasC)
                p.mm(PV, PV[0:64, :], vh, vh[:, kc * DK:(kc + 1) * DK],
                     EE, EE.ap(), start=(kc == 0), stop=(kc == nkc - 1))
                p.mm(DEN, DEN[0:1, :], C["ones_bf"],
                     C["ones_bf"].h.ap()[:, 0:1], EE, EE.ap(),
                     start=(kc == 0), stop=(kc == nkc - 1))
            p.recip(rdf, rdf[0:1, 0:512], DEN, DEN[0:1, :])
            p.copy(rden, rden[0:1, 0:512], rdf, rdf[0:1, 0:512])
            p.mm(p.PB[3], p.PB[3][0:64, :], onesrow, onesrow[0:1, 0:64],
                 rden, rden[0:1, 0:512])
            p.copy(ebc, ebc[0:64, 0:512], p.PB[3], p.PB[3][0:64, :])
            p.copy(EE, EE[0:64, :], PV, PV[0:64, :])
            p.tt(osb, osb.ap(), EE, EE[0:64, :], ebc, ebc[0:64, 0:512],
                 ALU.mult)
            p.dma(obuf, obuf[h * 64:(h + 1) * 64, qw * 512:(qw + 1) * 512],
                  osb, osb.ap(), eng="gpsimd")

    wo = load_weight_lhsT_slice(p, "l0a_woT", wo_h, "l0a_wT", 0, 1024, C,
                                tmp=stg)
    oc_in = qT
    for tq in range(T // 512):
        for c in range(NCH):
            p.dma(oc_in, oc_in[:, c * 512:(c + 1) * 512], obuf,
                  obuf[c * 128:(c + 1) * 128, tq * 512:(tq + 1) * 512],
                  eng="sync")
        for oc in range(NCH):
            pb = p.PB[oc % 2]
            for c in range(NCH):
                p.mm(pb, pb.ap(), wo, wtile(wo, 1024, c, oc * 128,
                                            (oc + 1) * 128),
                     oc_in, oc_in[:, c * 512:(c + 1) * 512],
                     start=(c == 0), stop=(c == NCH - 1))
            p.ts(fout, fout[:, oc * T + tq * 512:oc * T + tq * 512 + 512],
                 pb, pb.ap(), bqk[:, 16 + oc:16 + oc + 1],
                 ALU.add, scalar_buf=bqk)



def emit_ln_general(p, dst, src, C, gb_col, gb_buf, eps, T=SEQ, src_bf=False):
    S1v = p.PBW[:, 0:512]
    S2v = p.PBW[:, 512:1024]
    SS = p.PBW
    ysq = p.SCR[0]
    A = p.SCR[1]
    Bb = p.SCR[2]
    ones_f32 = C["ones_f32"]
    ones_bf = C["ones_bf"]
    ntile = T // 512
    for t in range(ntile):
        for c in range(NCH):
            csl = slice(c * T + t * 512, c * T + t * 512 + 512)
            p.act(ysq, ysq.ap(), src, src[:, csl], AF.Square)
            if src_bf:
                p.mm(SS, S1v, ones_bf, ones_bf.h.ap(), src, src[:, csl],
                     start=(c == 0), stop=(c == NCH - 1))
            else:
                p.mm(SS, S1v, ones_f32, ones_f32.ap(), src, src[:, csl],
                     start=(c == 0), stop=(c == NCH - 1))
            p.mm(SS, S2v, ones_f32, ones_f32.ap(), ysq, ysq.ap(),
                 start=(c == 0), stop=(c == NCH - 1))
        p.ts(A, A.ap(), SS, S1v, 1.0 / DM, ALU.mult)
        p.ts(Bb, Bb.ap(), SS, S2v, 1.0 / DM, ALU.mult)
        p.act(ysq, ysq.ap(), A, A.ap(), AF.Square)
        p.tt(Bb, Bb.ap(), Bb, Bb.ap(), ysq, ysq.ap(), ALU.subtract)
        p.act(Bb, Bb.ap(), Bb, Bb.ap(), AF.Sqrt, bias=eps)
        p.recip(Bb, Bb.ap(), Bb, Bb.ap())
        p.tt(A, A.ap(), A, A.ap(), Bb, Bb.ap(), ALU.mult)
        for c in range(NCH):
            csl = slice(c * T + t * 512, c * T + t * 512 + 512)
            p.tt(ysq, ysq.ap(), src, src[:, csl], Bb, Bb.ap(), ALU.mult)
            p.tt(ysq, ysq.ap(), ysq, ysq.ap(), A, A.ap(), ALU.subtract)
            p.emit("vector", lambda E, c=c, csl=csl: E.tensor_scalar(
                dst[:, csl], ysq.h.ap(),
                gb_buf[:, gb_col + 2 * c:gb_col + 2 * c + 1],
                gb_buf[:, gb_col + 2 * c + 1:gb_col + 2 * c + 2],
                ALU.mult, ALU.add),
                outs=[dst], ins=[ysq, gb_buf])


def body_l1_attn(p, stream, fout, W, C, T=SEQ):
    H, DH = 16, 64
    KEPS = 0.001
    wq_h = p.const("l1a_wq", W["L1_attn_wq"].astype(np.float32).reshape(1024, 1024))
    wk_h = p.const("l1a_wk", W["L1_attn_wk"].astype(np.float32).reshape(1024, 1024))
    wv_h = p.const("l1a_wv", W["L1_attn_wv"].astype(np.float32).reshape(1024, 1024))
    wo_h = p.const("l1a_wo", W["L1_attn_wo"].astype(np.float32).reshape(1024, 1024))
    w1_h = p.const("l1a_w1", W["L1_attn_w1"].astype(np.float32))
    w2_h = p.const("l1a_w2", W["L1_attn_w2"].astype(np.float32))
    def gb(scale, bias):
        g = scale.reshape(8, 128).T
        b = bias.reshape(8, 128).T
        out = np.empty((128, 16), np.float32)
        out[:, 0::2] = g
        out[:, 1::2] = b
        return out
    lnw = sbconst(p, "l1a_lnw", np.concatenate(
        [gb(W["L1_attn_ln1_scale"], W["L1_attn_ln1_bias"]),
         gb(W["L1_attn_ln2_scale"], W["L1_attn_ln2_bias"]),
         W["L1_attn_b2"].astype(np.float32).reshape(8, 128).T], axis=1))
    b1c = sbconst(p, "l1a_b1", W["L1_attn_b1"].astype(np.float32)
                  .reshape(32, 128).T)
    tri = np.triu(np.ones((128, 128), dtype=BF))
    triM = sbconst(p, "l1a_tri", tri)
    onesrow = sbconst(p, "l1a_ones1p", np.ones((1, 128), dtype=BF))

    xln = p.sb("l1a_xln", [128, NCH * T], BF16)
    emit_ln_general(p, xln, stream, C, 0, lnw, 1e-6)
    qT = p.sb("l1a_qT", [128, NCH * T], BF16)
    wbuf = p.sb("l1a_wbuf", [128, NCH * 1024], BF16)
    ktok = p.dram("l1a_ktok", [T, 1024], BF16)
    vtok = p.dram("l1a_vtok", [T, 1024], BF16)
    obuf = p.dram("l1a_obuf", [1024, T], BF16)
    evac = p.sb("l1a_evac", [128, 1024], BF16)
    for c in range(NCH):
        p.dma(wbuf, wbuf[:, c * 1024:(c + 1) * 1024], wq_h,
              wq_h[c * 128:(c + 1) * 128, :], eng="gpsimd")
    for oc in range(NCH):
        for tq in range(T // 512):
            pb = p.PB[tq % 2]
            for c in range(NCH):
                p.mm(pb, pb.ap(), wbuf,
                     wbuf[:, c * 1024 + oc * 128:c * 1024 + (oc + 1) * 128],
                     xln, xln[:, c * T + tq * 512:c * T + tq * 512 + 512],
                     start=(c == 0), stop=(c == NCH - 1))
            p.ts(qT, qT[:, oc * T + tq * 512:oc * T + tq * 512 + 512],
                 pb, pb.ap(), 0.0, ALU.max)
    for c in range(NCH):
        p.ts(qT, qT[:, c * T:(c + 1) * T], qT, qT[:, c * T:(c + 1) * T],
             KEPS, ALU.add)
    for wh, dst, isk in ((wk_h, ktok, True), (wv_h, vtok, False)):
        for c in range(NCH):
            p.dma(wbuf, wbuf[:, c * 1024:(c + 1) * 1024], wh,
                  wh[c * 128:(c + 1) * 128, :], eng="gpsimd")
        for tt in range(T // 128):
            for half in range(2):
                pv = p.PBW
                sl = slice(half * 512, (half + 1) * 512)
                for c in range(NCH):
                    p.mm(pv, pv[:, sl], xln,
                         xln[:, c * T + tt * 128:c * T + (tt + 1) * 128],
                         wbuf, wbuf[:, c * 1024 + half * 512:
                                    c * 1024 + (half + 1) * 512],
                         start=(c == 0), stop=(c == NCH - 1))
            if isk:
                p.ts(evac, evac.ap(), p.PBW, p.PBW.ap(), 0.0, ALU.max)
                p.ts(evac, evac.ap(), evac, evac.ap(), KEPS, ALU.add)
            else:
                p.copy(evac, evac.ap(), p.PBW, p.PBW.ap())
            p.dma(dst, dst.h.ap()[tt * 128:(tt + 1) * 128, :], evac,
                  evac.ap(), eng="gpsimd")

    phiqh = p.sb("l1a_phiqh", [64, T], BF16)
    vh = p.sb("l1a_vh", [128, (T // 128) * 65], BF16)
    kh = p.sb("l1a_kh", [128, (T // 128) * DH], BF16)
    kmT = p.sb("l1a_kmT", [64, 128], BF16)
    Sbf = p.sb("l1a_Sbf", [64, 65], BF16)
    Am = p.sb("l1a_Am", [128, 128], BF16)
    attnh = p.sb("l1a_attnh", [64, T], BF16)
    rdf = p.SCR[0]
    rden = p.sb("l1a_rden", [1, 128], BF16)
    ebc = p.sb("l1a_ebc", [64, 128], BF16)
    for h in range(H):
        hc, hb = h // 2, (h % 2) * 64
        p.dma(phiqh, phiqh.ap(), qT, qT[hb:hb + 64, hc * T:(hc + 1) * T],
              eng="sync")
        p.memset(vh, vh.ap(), 1.0)
        for quarter in range(4):
            p.dma(vh, vh.h.ap()[:, quarter * 4 * 65:(quarter + 1) * 4 * 65]
                  .rearrange("p (a e) -> p a e", e=65)[:, :, 0:64],
                  vtok, vtok.h.ap()[quarter * 512:(quarter + 1) * 512,
                                    h * 64:(h + 1) * 64]
                  .rearrange("(a p) d -> p a d", p=128), eng="sync")
            p.dma(kh, kh.h.ap()[:, quarter * 4 * DH:(quarter + 1) * 4 * DH]
                  .rearrange("p (a d) -> p a d", d=DH),
                  ktok, ktok.h.ap()[quarter * 512:(quarter + 1) * 512,
                                    h * 64:(h + 1) * 64]
                  .rearrange("(a p) d -> p a d", p=128), eng="sync")
        ST = p.PB[0]
        for cch in range(T // 128):
            p.transp(p.PBbf, p.PBbf[0:64, 0:128], kh,
                     kh[:, cch * DH:(cch + 1) * DH], C["iden"])
            p.copy(kmT, kmT.ap(), p.PBbf, p.PBbf[0:64, 0:128])
            AT = p.PB[1]
            p.mm(AT, AT[:, 0:128], kmT, kmT.ap(), phiqh,
                 phiqh[:, cch * 128:(cch + 1) * 128])
            p.tt(Am, Am.ap(), AT, AT[:, 0:128], triM, triM.h.ap(), ALU.mult)
            if cch > 0:
                p.copy(Sbf, Sbf.ap(), ST, ST[0:64, 0:65])
            OU = p.PB[2]
            p.mm(OU, OU[0:65, 0:128], vh, vh[:, cch * 65:(cch + 1) * 65],
                 Am, Am.ap(), start=True, stop=(cch == 0))
            if cch > 0:
                p.mm(OU, OU[0:65, 0:128], Sbf, Sbf.ap(), phiqh,
                     phiqh[:, cch * 128:(cch + 1) * 128],
                     start=False, stop=True)
            p.mm(ST, ST[0:64, 0:65], kh, kh[:, cch * DH:(cch + 1) * DH],
                 vh, vh[:, cch * 65:(cch + 1) * 65],
                 start=(cch == 0), stop=(cch == T // 128 - 1))
            p.recip(rdf, rdf[0:1, 0:128], OU, OU[64:65, 0:128])
            p.copy(rden, rden.ap(), rdf, rdf[0:1, 0:128])
            p.mm(p.PB[3], p.PB[3][0:64, 0:128], onesrow, onesrow[0:1, 0:64],
                 rden, rden.ap())
            p.copy(ebc, ebc.ap(), p.PB[3], p.PB[3][0:64, 0:128])
            p.copy(Am, Am[0:64, 0:128], OU, OU[0:64, 0:128])
            p.tt(attnh, attnh[:, cch * 128:(cch + 1) * 128], Am,
                 Am[0:64, 0:128], ebc, ebc.ap(), ALU.mult)
        p.dma(obuf, obuf[hc * 128 + hb:hc * 128 + hb + 64, :], attnh,
              attnh.ap(), eng="gpsimd")

    for c in range(NCH):
        p.dma(wbuf, wbuf[:, c * 1024:(c + 1) * 1024], wo_h,
              wo_h[c * 128:(c + 1) * 128, :], eng="gpsimd")
    oc_in = qT
    for tq in range(T // 512):
        for c in range(NCH):
            p.dma(oc_in, oc_in[:, c * 512:(c + 1) * 512], obuf,
                  obuf[c * 128:(c + 1) * 128, tq * 512:(tq + 1) * 512],
                  eng="sync")
        for oc in range(NCH):
            pb = p.PB[oc % 2]
            for c in range(NCH):
                p.mm(pb, pb.ap(), wbuf,
                     wbuf[:, c * 1024 + oc * 128:c * 1024 + (oc + 1) * 128],
                     oc_in, oc_in[:, c * 512:(c + 1) * 512],
                     start=(c == 0), stop=(c == NCH - 1))
            p.tt(fout, fout[:, oc * T + tq * 512:oc * T + tq * 512 + 512],
                 pb, pb.ap(), stream,
                 stream[:, oc * T + tq * 512:oc * T + tq * 512 + 512],
                 ALU.add)

    emit_ln_general(p, xln, fout, C, 16, lnw, 1e-6, src_bf=True)
    hid = qT
    w1s = p.sb("l1a_w1s", [128, NCH * 512], BF16)
    w2s = w1s
    for tq in range(4):
        t0 = tq * 512
        for hs in range(8):
            for c in range(NCH):
                p.dma(w1s, w1s[:, c * 512:(c + 1) * 512], w1_h,
                      w1_h[c * 128:(c + 1) * 128, hs * 512:(hs + 1) * 512],
                      eng="gpsimd")
            for hq in range(4):
                pb = p.PB[hq % 2]
                for c in range(NCH):
                    p.mm(pb, pb.ap(), w1s,
                         w1s[:, c * 512 + hq * 128:c * 512 + (hq + 1) * 128],
                         xln, xln[:, c * T + t0:c * T + t0 + 512],
                         start=(c == 0), stop=(c == NCH - 1))
                hcc = hs * 4 + hq
                p.act(hid, hid[:, hcc * 512:(hcc + 1) * 512], pb, pb.ap(),
                      AF.Gelu_apprx_tanh, bias=b1c[:, hcc:hcc + 1],
                      bias_buf=b1c)
        for ocg in range(2):
            for hcc in range(32):
                p.dma(w2s, w2s[:, 0:1024], w2_h,
                      w2_h[hcc * 128:(hcc + 1) * 128, :], eng="gpsimd")
                for oq in range(4):
                    oc = ocg * 4 + oq
                    pb = p.PB[oq]
                    p.mm(pb, pb.ap(), w2s,
                         w2s[:, oc * 128:(oc + 1) * 128],
                         hid, hid[:, hcc * 512:(hcc + 1) * 512],
                         start=(hcc == 0), stop=(hcc == 31))
                    if hcc == 31:
                        csl = slice(oc * T + t0, oc * T + t0 + 512)
                        p.ts(p.SCR[0], p.SCR[0].ap(), pb, pb.ap(),
                             lnw[:, 32 + oc:32 + oc + 1], ALU.add,
                             scalar_buf=lnw)
                        p.tt(fout, fout[:, csl], fout, fout[:, csl],
                             p.SCR[0], p.SCR[0].ap(), ALU.add)



def body_l3_ffn(p, stream, fout, W, C, T=SEQ):
    H, DH, WIN = 16, 64, 256
    NL = 6
    wqkv_h = p.const("l3f_wqkv", W["L3_ffn_Wqkv"].astype(np.float32))
    wo_h = p.const("l3f_wo", W["L3_ffn_Wo"].astype(np.float32))
    w1_h = p.const("l3f_w1", W["L3_ffn_W1"].astype(np.float32))
    w2_h = p.const("l3f_w2", W["L3_ffn_W2"].astype(np.float32))
    def stackcols(arr):
        return np.ascontiguousarray(
            arr.reshape(NL, 8, 128).transpose(2, 0, 1).reshape(128, NL * 8)
        ).astype(np.float32)
    lng = sbconst(p, "l3f_lng", stackcols(W["L3_ffn_ln_g"]))
    lnb = sbconst(p, "l3f_lnb", stackcols(W["L3_ffn_ln_b"]))
    b2c = sbconst(p, "l3f_b2", stackcols(W["L3_ffn_b2"]))
    b1pad = np.zeros((NL, 2, 22 * 128), np.float32)
    b1pad[:, 0, :2730] = W["L3_ffn_b1"][:, :2730]
    b1pad[:, 1, :2730] = W["L3_ffn_b1"][:, 2730:]
    b1c = sbconst(p, "l3f_b1", np.ascontiguousarray(
        b1pad.reshape(NL * 2 * 22, 128).T))
    qsd = np.tile(W["L3_ffn_q_scale"].astype(np.float32).T, (2, 1))
    ksd = np.tile(W["L3_ffn_k_scale"].astype(np.float32).T, (2, 1))
    qsC = sbconst(p, "l3f_qs", qsd)
    ksC = sbconst(p, "l3f_ks", ksd)
    half = np.arange(0, DH, 2, dtype=np.float64)
    invf = 1.0 / (10000.0 ** (half / DH))
    t_ = np.arange(T, dtype=np.float64)
    fr = t_[:, None] * invf[None, :]
    base_scale = (half + 0.4 * DH) / (1.4 * DH)
    power = (t_ - T // 2) / (WIN // 2)
    scl = (base_scale[None, :] ** power[:, None]).T
    cosf = np.cos(fr).T
    sinf = np.sin(fr).T
    qcos = np.tile(cosf * scl, (4, 1)) * 8.0
    qsin = np.concatenate([-sinf * scl, sinf * scl] * 2, axis=0) * 8.0
    kcos = np.tile(cosf / scl, (4, 1))
    ksin = np.concatenate([-sinf / scl, sinf / scl] * 2, axis=0)
    qcosC = sbconst(p, "l3f_qcos", qcos.astype(BF))
    qsinC = sbconst(p, "l3f_qsin", qsin.astype(BF))
    kcosC = sbconst(p, "l3f_kcos", kcos.astype(BF))
    ksinC = sbconst(p, "l3f_ksin", ksin.astype(BF))
    DS = (-256, -128, 0, 128, 256, 384)
    mlist = []
    for d in DS:
        pp = np.arange(128)[:, None]
        f = np.arange(512)[None, :]
        kk = d + pp
        valid = (kk <= f) & (f - kk <= 256)
        mlist.append(np.where(valid, 0.0, -1e9).astype(BF))
    maskC = sbconst(p, "l3f_mask", np.concatenate(mlist, axis=1))
    hselE = np.zeros((128, 2), np.float32)
    hselE[0:64, 0] = 1
    hselE[64:128, 1] = 1
    hselC = sbconst(p, "l3f_hsel", hselE)
    hselB = sbconst(p, "l3f_hselb", hselE.astype(BF))
    onesrow = sbconst(p, "l3f_ones1p", np.ones((1, 128), dtype=BF))

    NL = globals().get("L3F_NL", NL)
    SKIP_FFN = globals().get("L3F_SKIP_FFN", False)
    SKIP_ATTN = globals().get("L3F_SKIP_ATTN", False)
    acc = p.dram("l3f_acc", [1024, T], F32)
    vtok = p.dram("l3f_vtok", [T, 1024], BF16)
    obuf = p.dram("l3f_obuf", [1024, T], BF16)
    hbuf = p.dram("l3f_hbuf", [22 * 128, T], BF16)
    curS = p.sb("l3f_curS", [128, NCH * T], BF16)
    wsl = p.sb("l3f_wsl", [128, NCH * 512], BF16)
    wsl2 = p.sb("l3f_wsl2", [128, NCH * 512], BF16)
    qc = p.sb("l3f_qc", [128, T], BF16)
    kc = p.sb("l3f_kc", [128, T], BF16)
    swp = p.sb("l3f_swp", [128, T], BF16)
    qh2 = p.sb("l3f_qh2", [128, T], BF16)
    kh2 = p.sb("l3f_kh2", [128, T], BF16)
    vh = p.sb("l3f_vh", [128, (T // 128) * DH], BF16)
    hch = p.sb("l3f_hch", [128, 512], BF16)
    EEc = p.sb("l3f_EE", [128, 512], BF16)
    osb = p.sb("l3f_osb", [64, 512], BF16)
    ebc = p.sb("l3f_ebc", [64, 512], BF16)
    rden = p.sb("l3f_rden", [1, 1024], BF16)

    def build_curS(first):
        for c in range(NCH):
            for tq in range(4):
                t0 = tq * 512
                csl = slice(c * T + t0, c * T + t0 + 512)
                if not first:
                    p.dma(p.SCR[1], p.SCR[1].ap(), acc,
                          acc[c * 128:(c + 1) * 128, t0:t0 + 512], eng="sync")
                if c < 4:
                    if first:
                        p.copy(curS, curS[:, csl], stream, stream[:, csl])
                    else:
                        p.tt(curS, curS[:, csl], stream, stream[:, csl],
                             p.SCR[1], p.SCR[1].ap(), ALU.add)
                else:
                    w = 511 if tq == 3 else 512
                    dsl = slice(c * T + t0 + 1, c * T + t0 + 1 + w)
                    if first:
                        p.copy(curS, curS[:, dsl], stream,
                               stream[:, c * T + t0:c * T + t0 + w])
                    else:
                        p.tt(curS, curS[:, dsl], stream,
                             stream[:, c * T + t0:c * T + t0 + w],
                             p.SCR[1], p.SCR[1][:, 0:w], ALU.add)
            if c >= 4:
                p.memset(curS, curS[:, c * T:c * T + 1], 0.0)

    def acc_update(oc, tq, pb, pb_ap, bias_col=None, bias_buf=None,
                   first=False):
        t0 = tq * 512
        if bias_col is not None:
            p.ts(p.SCR[0], p.SCR[0].ap(), pb, pb_ap, bias_col, ALU.add,
                 scalar_buf=bias_buf)
        else:
            p.copy(p.SCR[0], p.SCR[0].ap(), pb, pb_ap)
        if not first:
            p.dma(p.SCR[1], p.SCR[1].ap(), acc,
                  acc[oc * 128:(oc + 1) * 128, t0:t0 + 512], eng="sync")
            p.tt(p.SCR[0], p.SCR[0].ap(), p.SCR[0], p.SCR[0].ap(),
                 p.SCR[1], p.SCR[1].ap(), ALU.add)
        p.dma(acc, acc[oc * 128:(oc + 1) * 128, t0:t0 + 512],
              p.SCR[0], p.SCR[0].ap(), eng="gpsimd")

    for li in range(NL):
        build_curS(first=(li == 0))
        for c in range(NCH):
            p.dma(wsl, wsl[:, c * 512:(c + 1) * 512], wqkv_h,
                  wqkv_h.h.ap()[li, c * 128:(c + 1) * 128, 2048:2560],
                  eng="gpsimd")
            p.dma(wsl2, wsl2[:, c * 512:(c + 1) * 512], wqkv_h,
                  wqkv_h.h.ap()[li, c * 128:(c + 1) * 128, 2560:3072],
                  eng="gpsimd")
        for tt in range(T // 128):
            for hf, wb_ in ((0, wsl), (1, wsl2)):
                pv = p.PBW
                sl = slice(hf * 512, (hf + 1) * 512)
                for c in range(NCH):
                    p.mm(pv, pv[:, sl], curS,
                         curS[:, c * T + tt * 128:c * T + (tt + 1) * 128],
                         wb_, wb_[:, c * 512:(c + 1) * 512],
                         start=(c == 0), stop=(c == NCH - 1))
            p.copy(EEc, EEc.ap(), p.PBW, p.PBW[:, 0:512])
            p.dma(vtok, vtok[tt * 128:(tt + 1) * 128, 0:512], EEc,
                  EEc.ap(), eng="gpsimd")
            p.copy(EEc, EEc.ap(), p.PBW, p.PBW[:, 512:1024])
            p.dma(vtok, vtok[tt * 128:(tt + 1) * 128, 512:1024], EEc,
                  EEc.ap(), eng="gpsimd")
        p.memset(kh2, kh2[64:65, :], 1.0)
        for cc in range(NCH):
            for dst, col0, scC, cosC, sinC in (
                    (qc, 0, qsC, qcosC, qsinC),
                    (kc, 1024, ksC, kcosC, ksinC)):
                for c in range(NCH):
                    p.dma(wsl, wsl[:, c * 512:c * 512 + 128], wqkv_h,
                          wqkv_h.h.ap()[li, c * 128:(c + 1) * 128,
                                        col0 + cc * 64:col0 + cc * 64 + 64]
                          if False else
                          wqkv_h.h.ap()[li, c * 128:(c + 1) * 128,
                                        col0 + cc * 128:col0 + (cc + 1) * 128],
                          eng="gpsimd")
                for tq in range(4):
                    pb = p.PB[tq % 2]
                    for c in range(NCH):
                        p.mm(pb, pb.ap(), wsl,
                             wsl[:, c * 512:c * 512 + 128],
                             curS, curS[:, c * T + tq * 512:
                                        c * T + tq * 512 + 512],
                             start=(c == 0), stop=(c == NCH - 1))
                    p.act(p.SCR[0], p.SCR[0].ap(), pb, pb.ap(), AF.Square)
                    n0, n1 = p.PB[2], p.PB[3]
                    p.mm(n0, n0[0:1, :], hselC, hselC.ap()[:, 0:1],
                         p.SCR[0], p.SCR[0].ap())
                    p.mm(n1, n1[0:1, :], hselC, hselC.ap()[:, 1:2],
                         p.SCR[0], p.SCR[0].ap())
                    for half_i, nb in ((0, n0), (1, n1)):
                        p.act(p.SCR[1], p.SCR[1][0:1, 0:512], nb,
                              nb[0:1, :], AF.Sqrt)
                        p.ts(p.SCR[1], p.SCR[1][0:1, 0:512], p.SCR[1],
                             p.SCR[1][0:1, 0:512], 1e-12, ALU.max)
                        p.recip(p.SCR[1], p.SCR[1][0:1, 0:512], p.SCR[1],
                                p.SCR[1][0:1, 0:512])
                        p.copy(rden, rden[0:1, half_i * 512:
                                          half_i * 512 + 512],
                               p.SCR[1], p.SCR[1][0:1, 0:512])
                    ebc2 = p.sb("l3f_ebc2", [128, 512], BF16)
                    for half_i in range(2):
                        bcp = p.PB[2] if half_i == 0 else p.PB[3]
                        p.mm(bcp, bcp.ap(), onesrow, onesrow[0:1, 0:128],
                             rden, rden[0:1, half_i * 512:half_i * 512 + 512])
                        p.copy(ebc2, ebc2.ap(), bcp, bcp.ap())
                        p.tt(dst, dst[half_i * 64:(half_i + 1) * 64,
                                      tq * 512:(tq + 1) * 512],
                             pb, pb.ap()[half_i * 64:(half_i + 1) * 64, :],
                             ebc2, ebc2[half_i * 64:(half_i + 1) * 64, :],
                             ALU.mult)
                p.ts(dst, dst.ap(), dst, dst.ap(),
                     scC[:, li:li + 1], ALU.mult, scalar_buf=scC)
                for blk in range(4):
                    swb = blk ^ 1
                    p.dma(swp, swp[blk * 32:(blk + 1) * 32, :], dst,
                          dst[swb * 32:(swb + 1) * 32, :], eng="gpsimd")
                p.tt(swp, swp.ap(), swp, swp.ap(), sinC, sinC.h.ap(),
                     ALU.mult)
                p.tt(dst, dst.ap(), dst, dst.ap(), cosC, cosC.h.ap(),
                     ALU.mult)
                p.tt(dst, dst.ap(), dst, dst.ap(), swp, swp.ap(), ALU.add)
            for hp in range(2):
                h = cc * 2 + hp
                hb = hp * 64
                p.dma(qh2, qh2[0:64, :], qc, qc[hb:hb + 64, :], eng="sync")
                p.dma(kh2, kh2[0:64, :], kc, kc[hb:hb + 64, :], eng="sync")
                p.tt(swp, swp[0:64, :], qh2, qh2[0:64, :], kh2,
                     kh2[0:64, :], ALU.mult)
                for tq in range(4):
                    mb = p.PB[2]
                    p.mm(mb, mb[0:1, :], hselB, hselB.ap()[:, 0:1],
                         swp, swp[:, tq * 512:(tq + 1) * 512])
                    p.ts(p.SCR[2], p.SCR[2][0:1, 0:512], mb,
                         mb[0:1, :], -1.0, ALU.mult)
                    p.dma(qh2, qh2[64:65, tq * 512:(tq + 1) * 512],
                          p.SCR[2], p.SCR[2][0:1, 0:512], eng="gpsimd")
                for quarter in range(4):
                    p.dma(vh, vh.h.ap()[:, quarter * 4 * DH:
                                        (quarter + 1) * 4 * DH]
                          .rearrange("p (a d) -> p a d", d=DH),
                          vtok, vtok.h.ap()[quarter * 512:(quarter + 1) * 512,
                                            h * 64:(h + 1) * 64]
                          .rearrange("(a p) d -> p a d", p=128), eng="sync")
                for qt in range(4):
                    kc_lo = max(0, 4 * qt - 2)
                    kc_hi = 4 * qt + 4
                    PV, DEN = p.PB[0], p.PB[1]
                    nk = kc_hi - kc_lo
                    for ki, kcx in enumerate(range(kc_lo, kc_hi)):
                        d = kcx * 128 - qt * 512
                        di = DS.index(d)
                        S = p.PB[2]
                        p.mm(S, S.ap(), kh2,
                             kh2[0:65, kcx * 128:(kcx + 1) * 128],
                             qh2, qh2[0:65, qt * 512:(qt + 1) * 512])
                        p.tt(S, S.ap(), S, S.ap(), maskC,
                             maskC.h.ap()[:, di * 512:(di + 1) * 512],
                             ALU.add)
                        p.act(EEc, EEc.ap(), S, S.ap(), AF.Exp)
                        p.mm(PV, PV[0:64, :], vh,
                             vh[:, kcx * DH:(kcx + 1) * DH], EEc, EEc.ap(),
                             start=(ki == 0), stop=(ki == nk - 1))
                        p.mm(DEN, DEN[0:1, :], C["ones_bf"],
                             C["ones_bf"].h.ap()[:, 0:1], EEc, EEc.ap(),
                             start=(ki == 0), stop=(ki == nk - 1))
                    p.recip(p.SCR[0], p.SCR[0][0:1, 0:512], DEN,
                            DEN[0:1, :])
                    p.copy(rden, rden[0:1, 0:512], p.SCR[0],
                           p.SCR[0][0:1, 0:512])
                    p.mm(p.PB[3], p.PB[3][0:64, :], onesrow,
                         onesrow[0:1, 0:64], rden, rden[0:1, 0:512])
                    p.copy(ebc, ebc.ap(), p.PB[3], p.PB[3][0:64, :])
                    p.copy(EEc, EEc[0:64, :], PV, PV[0:64, :])
                    p.tt(osb, osb.ap(), EEc, EEc[0:64, :], ebc, ebc.ap(),
                         ALU.mult)
                    p.dma(obuf, obuf[h * 64:(h + 1) * 64,
                                     qt * 512:(qt + 1) * 512],
                          osb, osb.ap(), eng="gpsimd")
        for c in range(NCH):
            p.dma(wsl, wsl[:, c * 512:(c + 1) * 512], wo_h,
                  wo_h.h.ap()[li, c * 128:(c + 1) * 128, 0:512], eng="gpsimd")
            p.dma(wsl2, wsl2[:, c * 512:(c + 1) * 512], wo_h,
                  wo_h.h.ap()[li, c * 128:(c + 1) * 128, 512:1024],
                  eng="gpsimd")
        for tq in range(4):
            for c in range(NCH):
                p.dma(curS, curS[:, c * T + tq * 512:c * T + tq * 512 + 512],
                      obuf, obuf[c * 128:(c + 1) * 128,
                                 tq * 512:(tq + 1) * 512], eng="sync")
            for oc in range(NCH):
                pb = p.PB[oc % 2]
                wb_ = wsl if oc < 4 else wsl2
                ocx = oc % 4
                for c in range(NCH):
                    p.mm(pb, pb.ap(), wb_,
                         wb_[:, c * 512 + ocx * 128:c * 512 + (ocx + 1) * 128],
                         curS, curS[:, c * T + tq * 512:c * T + tq * 512 + 512],
                         start=(c == 0), stop=(c == NCH - 1))
                acc_update(oc, tq, pb, pb.ap(), first=(li == 0))
        if SKIP_FFN:
            continue
        build_curS(first=False)
        emit_ln_general2(p, curS, curS, C, lng, lnb, li * 8, 1e-5,
                         src_bf=True)
        for hs in range(22):
            hw = 128 if hs < 21 else 42
            for part, wb_ in ((0, wsl), (1, wsl2)):
                col0 = part * 2730
                for c in range(NCH):
                    p.dma(wb_, wb_[:, c * 512:c * 512 + hw], w1_h,
                          w1_h.h.ap()[li, c * 128:(c + 1) * 128,
                                      col0 + hs * 128:col0 + hs * 128 + hw],
                          eng="gpsimd")
            for tq in range(4):
                pa, pg = p.PB[0], p.PB[1]
                for c in range(NCH):
                    csl = curS[:, c * T + tq * 512:c * T + tq * 512 + 512]
                    p.mm(pa, pa[0:hw, :], wsl, wsl[:, c * 512:c * 512 + hw],
                         curS, csl, start=(c == 0), stop=(c == NCH - 1))
                    p.mm(pg, pg[0:hw, :], wsl2,
                         wsl2[:, c * 512:c * 512 + hw],
                         curS, csl, start=(c == 0), stop=(c == NCH - 1))
                p.ts(p.SCR[0], p.SCR[0][0:hw, :], pa, pa[0:hw, :],
                     b1c.h.ap()[0:hw, li * 44 + hs:li * 44 + hs + 1],
                     ALU.add, scalar_buf=b1c)
                p.act(p.SCR[1], p.SCR[1][0:hw, :], pg, pg[0:hw, :], AF.Gelu,
                      bias=b1c.h.ap()[0:hw, li * 44 + 22 + hs:
                                      li * 44 + 22 + hs + 1],
                      bias_buf=b1c)
                p.tt(EEc, EEc[0:hw, :], p.SCR[0], p.SCR[0][0:hw, :],
                     p.SCR[1], p.SCR[1][0:hw, :], ALU.mult)
                p.dma(hbuf, hbuf[hs * 128:hs * 128 + hw,
                                 tq * 512:(tq + 1) * 512],
                      EEc, EEc[0:hw, :], eng="gpsimd")
        for tq in range(4):
            for ocg in range(2):
                for hs in range(22):
                    hw = 128 if hs < 21 else 42
                    p.dma(hch, hch[0:hw, :], hbuf,
                          hbuf[hs * 128:hs * 128 + hw,
                               tq * 512:(tq + 1) * 512], eng="sync")
                    p.dma(wsl, wsl[0:hw, 0:1024], w2_h,
                          w2_h.h.ap()[li, hs * 128:hs * 128 + hw, :],
                          eng="gpsimd")
                    for oq in range(4):
                        oc = ocg * 4 + oq
                        p.mm(p.PB[oq], p.PB[oq].ap(), wsl,
                             wsl.h.ap()[0:hw, oc * 128:(oc + 1) * 128],
                             hch, hch[0:hw, :],
                             start=(hs == 0), stop=(hs == 21))
                for oq in range(4):
                    oc = ocg * 4 + oq
                    acc_update(oc, tq, p.PB[oq], p.PB[oq].ap(),
                               bias_col=b2c[:, li * 8 + oc:li * 8 + oc + 1],
                               bias_buf=b2c)
    for c in range(NCH):
        for tq in range(4):
            t0 = tq * 512
            p.dma(p.SCR[1], p.SCR[1].ap(), acc,
                  acc[c * 128:(c + 1) * 128, t0:t0 + 512], eng="sync")
            p.tt(fout, fout[:, c * T + t0:c * T + t0 + 512], stream,
                 stream[:, c * T + t0:c * T + t0 + 512],
                 p.SCR[1], p.SCR[1].ap(), ALU.add)


def emit_ln_general2(p, dst, src, C, gbuf, bbuf, col0, eps, T=SEQ,
                     src_bf=False):
    S1v = p.PBW[:, 0:512]
    S2v = p.PBW[:, 512:1024]
    SS = p.PBW
    ysq = p.SCR[0]
    A = p.SCR[1]
    Bb = p.SCR[2]
    ones_f32 = C["ones_f32"]
    ones_bf = C["ones_bf"]
    for t in range(T // 512):
        for c in range(NCH):
            csl = slice(c * T + t * 512, c * T + t * 512 + 512)
            p.act(ysq, ysq.ap(), src, src[:, csl], AF.Square)
            if src_bf:
                p.mm(SS, S1v, ones_bf, ones_bf.h.ap(), src, src[:, csl],
                     start=(c == 0), stop=(c == NCH - 1))
            else:
                p.mm(SS, S1v, ones_f32, ones_f32.ap(), src, src[:, csl],
                     start=(c == 0), stop=(c == NCH - 1))
            p.mm(SS, S2v, ones_f32, ones_f32.ap(), ysq, ysq.ap(),
                 start=(c == 0), stop=(c == NCH - 1))
        p.ts(A, A.ap(), SS, S1v, 1.0 / DM, ALU.mult)
        p.ts(Bb, Bb.ap(), SS, S2v, 1.0 / DM, ALU.mult)
        p.act(ysq, ysq.ap(), A, A.ap(), AF.Square)
        p.tt(Bb, Bb.ap(), Bb, Bb.ap(), ysq, ysq.ap(), ALU.subtract)
        p.act(Bb, Bb.ap(), Bb, Bb.ap(), AF.Sqrt, bias=eps)
        p.recip(Bb, Bb.ap(), Bb, Bb.ap())
        p.tt(A, A.ap(), A, A.ap(), Bb, Bb.ap(), ALU.mult)
        for c in range(NCH):
            csl = slice(c * T + t * 512, c * T + t * 512 + 512)
            p.tt(ysq, ysq.ap(), src, src[:, csl], Bb, Bb.ap(), ALU.mult)
            p.tt(ysq, ysq.ap(), ysq, ysq.ap(), A, A.ap(), ALU.subtract)
            p.emit("vector", lambda E, csl=csl, c=c: E.tensor_scalar(
                dst[:, csl], ysq.h.ap(),
                gbuf[:, col0 + c:col0 + c + 1],
                bbuf[:, col0 + c:col0 + c + 1],
                ALU.mult, ALU.add),
                outs=[dst], ins=[ysq, gbuf, bbuf])




MOE_E, MOE_CAP, MOE_K, MOE_G, MOE_TG = 64, 512, 8, 8, 4
MOE_SCALE = 2.5
TPC = SEQ_PER_CORE * SEQ


def build_moe_route(p, W):
    C = std_consts(p)
    xin = p.dram("xin", [SEQ_PER_CORE, DM, SEQ], F32, kind="ExternalInput")
    rw_h = p.const("moe_rw", W["L0_ffn_router_w"].astype(np.float32))
    counts = p.dram("counts", [64, 1], F32, kind="ExternalOutput")
    cexp_d = p.dram("cexp", [64, TPC], F32, kind="ExternalOutput")
    excl_d = p.dram("excl", [64, TPC], F32, kind="ExternalOutput")
    mexp_d = p.dram("mexp", [64, TPC], F32, kind="ExternalOutput")
    w8_d = p.dram("w8", [TPC, 8], F32, kind="ExternalOutput")
    e8_d = p.dram("e8", [TPC, 8], U32, kind="ExternalOutput")
    xtok_d = p.dram("xtok", [TPC, DM], BF16, kind="ExternalOutput")
    stream = p.sb("stream", [128, NCH * SEQ], F32)
    sbf = p.sb("moe_sbf", [128, NCH * SEQ], BF16)
    rw = p.sb("moe_rwsb", [128, NCH * 64], F32)
    for c in range(NCH):
        p.dma(rw, rw[:, c * 64:(c + 1) * 64], rw_h,
              rw_h[c * 128:(c + 1) * 128, :], eng="sync")
    scr = p.sb("moe_scr", [128, 64], F32)
    gmax = p.sb("moe_gmax", [128, 8], F32)
    t8 = p.sb("moe_t8", [128, 8], F32)
    e8 = p.sb("moe_e8", [128, 8], U32)
    msk = p.sb("moe_msk", [128, 64], F32)
    rem = p.sb("moe_rem", [128, 64], F32)
    rno = p.sb("moe_rno", [128, 1], F32)
    ctok = p.sb("moe_ctok", [128, 64], F32)
    cexp = p.sb("moe_cexp", [64, TPC], F32)
    mexp = p.sb("moe_mexp", [64, TPC], F32)
    ones64 = p.sb("moe_ones64", [64, TPC], F32)
    vtile = p.sb("moe_vt", [128, 1024], BF16)
    for s in range(SEQ_PER_CORE):
        for c in range(NCH):
            p.dma(stream, stream[:, c * SEQ:(c + 1) * SEQ],
                  xin, xin[s, c * 128:(c + 1) * 128, :])
            p.copy(sbf, sbf[:, c * SEQ:(c + 1) * SEQ], stream,
                   stream[:, c * SEQ:(c + 1) * SEQ])
        for tt in range(SEQ // 128):
            gt = s * (SEQ // 128) + tt
            pb = p.PB[0]
            for c in range(NCH):
                p.mm(pb, pb[:, 0:64], stream,
                     stream[:, c * SEQ + tt * 128:c * SEQ + (tt + 1) * 128],
                     rw, rw[:, c * 64:(c + 1) * 64],
                     start=(c == 0), stop=(c == NCH - 1))
            p.act(scr, scr.ap(), pb, pb[:, 0:64], AF.Sigmoid)
            p.red(gmax, gmax.ap(), scr,
                  scr.h.ap().rearrange("p (g e) -> p g e", g=8), ALU.max)
            p.max8(t8, t8.ap(), gmax, gmax.ap())
            gsel = p.sb("moe_gsel", [128, 8], F32)
            p.ts(gsel, gsel.ap(), gmax, gmax.ap(), t8.h.ap()[:, 3:4],
                 ALU.is_ge, scalar_buf=t8)
            for g in range(8):
                p.ts(msk, msk.h.ap()[:, g * 8:(g + 1) * 8], scr,
                     scr.h.ap()[:, g * 8:(g + 1) * 8],
                     gsel.h.ap()[:, g:g + 1], ALU.mult, scalar_buf=gsel)
            p.max8(t8, t8.ap(), msk, msk.ap())
            p.max_index(e8, e8.ap(), t8, t8.ap(), msk, msk.ap())
            p.match_replace(rem, rem.ap(), t8, t8.ap(), msk, msk.ap(), 0.0)
            p.red(rno, rno.ap(), t8, t8.ap(), ALU.add)
            p.ts(rno, rno.ap(), rno, rno.ap(), 1e-9, ALU.max)
            p.recip(rno, rno.ap(), rno, rno.ap())
            p.ts(rno, rno.ap(), rno, rno.ap(), MOE_SCALE, ALU.mult)
            p.tt(ctok, ctok.ap(), msk, msk.ap(), rem, rem.ap(), ALU.subtract)
            p.ts(ctok, ctok.ap(), ctok, ctok.ap(), rno.h.ap()[:, 0:1],
                 ALU.mult, scalar_buf=rno)
            p.ts(t8, t8.ap(), t8, t8.ap(), rno.h.ap()[:, 0:1], ALU.mult,
                 scalar_buf=rno)
            p.dma(w8_d, w8_d.h.ap()[gt * 128:(gt + 1) * 128, :], t8,
                  t8.ap(), eng="gpsimd")
            p.dma(e8_d, e8_d.h.ap()[gt * 128:(gt + 1) * 128, :], e8,
                  e8.ap(), eng="gpsimd")
            ctb = p.sb("moe_ctb", [128, 64], BF16)
            p.copy(ctb, ctb.ap(), ctok, ctok.ap())
            p.transp(p.PBbf, p.PBbf[0:64, 0:128], ctb, ctb.ap(), C["iden"])
            p.copy(cexp, cexp[:, gt * 128:(gt + 1) * 128], p.PBbf,
                   p.PBbf[0:64, 0:128])
            for c in range(NCH):
                p.transp(p.PBbf, p.PBbf[:, 0:128], sbf,
                         sbf[:, c * SEQ + tt * 128:c * SEQ + (tt + 1) * 128],
                         C["iden"])
                p.copy(vtile, vtile[:, c * 128:(c + 1) * 128], p.PBbf,
                       p.PBbf[:, 0:128])
            p.dma(xtok_d, xtok_d.h.ap()[gt * 128:(gt + 1) * 128, :], vtile,
                  vtile.ap(), eng="gpsimd")
    p.emit("vector", lambda E: E.tensor_single_scalar(
        mexp.h.ap(), cexp.h.ap(), 0.0, ALU.is_gt), outs=[mexp], ins=[cexp])
    p.memset(ones64, ones64.ap(), 1.0)
    incl = ones64
    inclb = p.sb("moe_incl", [64, TPC], F32)
    p.scan(inclb, inclb.ap(), ones64, ones64.ap(), mexp, mexp.ap(), 0.0,
           ALU.mult, ALU.add)
    p.tt(ones64, ones64.ap(), inclb, inclb.ap(), mexp, mexp.ap(),
         ALU.subtract)
    p.dma(counts, counts.ap(), inclb, inclb[:, TPC - 1:TPC], eng="gpsimd")
    p.dma(cexp_d, cexp_d.ap(), cexp, cexp.ap(), eng="gpsimd")
    p.dma(excl_d, excl_d.ap(), ones64, ones64.ap(), eng="gpsimd")
    p.dma(mexp_d, mexp_d.ap(), mexp, mexp.ap(), eng="gpsimd")
    p.finalize_timing([counts, cexp_d, excl_d, mexp_d, w8_d, e8_d, xtok_d])
    return ["counts", "cexp", "excl", "mexp", "w8", "e8", "xtok"]


def build_moe_slots(p, W, trange=None):
    C = std_consts(p)
    counts_all = p.dram("counts_all", [8, 64], F32, kind="ExternalInput")
    coremask = p.dram("coremask", [8, 1], F32, kind="ExternalInput")
    tokid = p.dram("tokid", [TPC // 128, 128], F32, kind="ExternalInput")
    excl_d = p.dram("excl", [64, TPC], F32, kind="ExternalInput")
    mexp_d = p.dram("mexp", [64, TPC], F32, kind="ExternalInput")
    w8_d = p.dram("w8", [TPC, 8], F32, kind="ExternalInput")
    e8_d = p.dram("e8", [TPC, 8], U32, kind="ExternalInput")
    tw_d = p.dram("tw", [32768, 2], F32, kind="ExternalOutput")
    slotk_d = p.dram("slotk", [TPC, 8], F32, kind="ExternalOutput")
    ca = p.sb("mb_ca", [8, 64], F32)
    cm = p.sb("mb_cm", [8, 64], F32)
    p.dma(ca, ca.ap(), counts_all, counts_all.ap())
    p.dma(cm, cm[:, 0:1], coremask, coremask.ap())
    offs1 = p.PB[0]
    p.mm(offs1, offs1[0:1, 0:64], cm, cm.h.ap()[:, 0:1], ca, ca.ap())
    offsb = p.sb("mb_offs", [1, 64], F32)
    p.copy(offsb, offsb.ap(), offs1, offs1[0:1, 0:64])
    offp = p.sb("mb_offp", [64, 1], F32)
    p.dma(offp, offp.ap(), offsb, offsb.ap(), eng="gpsimd")
    excl = p.sb("mb_excl", [64, TPC], F32)
    mexp = p.sb("mb_mexp", [64, TPC], F32)
    p.dma(excl, excl.ap(), excl_d, excl_d.ap())
    p.dma(mexp, mexp.ap(), mexp_d, mexp_d.ap())
    ebase = sbconst(p, "mb_ebase",
                    (np.arange(64, dtype=np.float32) * MOE_CAP).reshape(64, 1))
    p.ts(excl, excl.ap(), excl, excl.ap(), offp.h.ap()[:, 0:1], ALU.add,
         scalar_buf=offp)
    vm = p.sb("mb_vm", [64, TPC], F32)
    p.emit("vector", lambda E: E.tensor_single_scalar(
        vm.h.ap(), excl.h.ap(), float(MOE_CAP), ALU.is_lt),
        outs=[vm], ins=[excl])
    p.tt(vm, vm.ap(), vm, vm.ap(), mexp, mexp.ap(), ALU.mult)
    p.ts(excl, excl.ap(), excl, excl.ap(), ebase.h.ap()[:, 0:1], ALU.add,
         scalar_buf=ebase)
    p.tt(excl, excl.ap(), excl, excl.ap(), vm, vm.ap(), ALU.mult)
    p.emit("vector", lambda E: E.tensor_scalar(
        mexp.h.ap(), vm.h.ap(), -1e9, 1e9, ALU.mult, ALU.add),
        outs=[mexp], ins=[vm])
    p.tt(excl, excl.ap(), excl, excl.ap(), mexp, mexp.ap(), ALU.add)
    e8t = p.sb("mb_e8t", [128, 8], F32)
    e8u = p.sb("mb_e8u", [128, 8], U32)
    w8t = p.sb("mb_w8t", [128, 8], F32)
    idt = p.sb("mb_idt", [128, 1], F32)
    stokb = p.sb("mb_stok", [128, 64], F32)
    sltb = p.sb("mb_slt", [64, 128], F32)
    ohot = p.sb("mb_ohot", [128, 64], F32)
    skt = p.sb("mb_skt", [128, 8], F32)
    iotaF = sbconst(p, "mb_iotaF", np.tile(
        np.arange(64, dtype=np.float32)[None, :], (128, 1)))
    pair = p.sb("mb_pair", [128, 2], F32)
    sint = p.sb("mb_sint", [128, 1], I32)
    scol = p.sb("mb_scol", [128, 1], F32)
    tlist = range(TPC // 128) if trange is None else range(*trange)
    for tt in tlist:
        p.dma(e8u, e8u.ap(), e8_d, e8_d.h.ap()[tt * 128:(tt + 1) * 128, :])
        p.copy(e8t, e8t.ap(), e8u, e8u.ap())
        p.dma(w8t, w8t.ap(), w8_d, w8_d.h.ap()[tt * 128:(tt + 1) * 128, :])
        p.dma(idt, idt.ap(), tokid, tokid.h.ap()[tt, :].rearrange(
            "(p one) -> p one", one=1))
        p.copy(pair, pair[:, 0:1], idt, idt.ap())
        p.copy(sltb, sltb.ap(), excl, excl[:, tt * 128:(tt + 1) * 128])
        iden32 = sbconst(p, "c_iden32", np.eye(128, dtype=np.float32))
        p.transp(p.PB[3], p.PB[3][:, 0:64], sltb,
                 sltb.h.ap()[0:64, 0:128], iden32,
                 iden_ap=iden32.h.ap()[0:64, 0:64])
        p.copy(stokb, stokb.ap(), p.PB[3], p.PB[3][:, 0:64])
        for k in range(8):
            p.ts(ohot, ohot.ap(), iotaF, iotaF.h.ap(), e8t.h.ap()[:, k:k + 1],
                 ALU.is_equal, scalar_buf=e8t)
            p.tt(ohot, ohot.ap(), ohot, ohot.ap(), stokb, stokb.ap(),
                 ALU.mult)
            p.red(scol, scol.ap(), ohot, ohot.ap(), ALU.add)
            p.copy(skt, skt[:, k:k + 1], scol, scol.ap())
            p.ts(scol, scol.ap(), scol, scol.ap(), 1.0e5, ALU.min)
            p.copy(sint, sint.ap(), scol, scol.ap())
            p.copy(pair, pair[:, 1:2], w8t, w8t[:, k:k + 1])
            p.iscatter(tw_d, tw_d.ap(), pair, pair.ap(), sint,
                       sint.ap(), 32767)
        p.dma(slotk_d, slotk_d.h.ap()[tt * 128:(tt + 1) * 128, :], skt,
              skt.ap(), eng="gpsimd")
    p.finalize_timing([tw_d, slotk_d])
    return ["tw", "slotk"]


def build_moe_expert(p, W):
    C = std_consts(p)
    NTW = 64
    tw_in = p.dram("tw8", [NTW, TPC, 2], F32, kind="ExternalInput")
    xall = p.dram("xall", [32768, DM], BF16, kind="ExternalInput")
    gw_h = p.dram("gw8", [8, DM, 512], F32, kind="ExternalInput")
    uw_h = p.dram("uw8", [8, DM, 512], F32, kind="ExternalInput")
    dw_h = p.dram("dw8", [8, 512, DM], F32, kind="ExternalInput")
    yout = p.dram("yout", [TPC, DM], F32, kind="ExternalOutput")
    tws = p.sb("mc_tws", [128, 64], F32)
    tid = p.sb("mc_tid", [128, 64], F32)
    for cr in range(NTW):
        p.dma(tws, tws.h.ap().rearrange("p (m f) -> p m f", f=2),
              tw_in, tw_in.h.ap()[cr, :, :].rearrange(
            "(m p) f -> p m f", p=128))
        if cr == 0:
            p.copy(tid, tid.ap(), tws, tws.ap())
        else:
            p.tt(tid, tid.ap(), tid, tid.ap(), tws, tws.ap(), ALU.add)
    tix = p.sb("mc_tix", [128, 32], I32)
    tfx = p.sb("mc_tfx", [128, 32], F32)
    p.emit("vector", lambda E: E.tensor_copy(
        tfx.h.ap(), tid.h.ap().rearrange("p (m f) -> p m f", f=2)[:, :, 0:1]
        .rearrange("p m one -> p (m one)")), outs=[tfx], ins=[tid])
    p.ts(tfx, tfx.ap(), tfx, tfx.ap(), 1.0e5, ALU.min)
    p.copy(tix, tix.ap(), tfx, tfx.ap())
    wcol = p.sb("mc_wcol", [128, 32], F32)
    p.emit("vector", lambda E: E.tensor_copy(
        wcol.h.ap(), tid.h.ap().rearrange("p (m f) -> p m f", f=2)[:, :, 1:2]
        .rearrange("p m one -> p (m one)")), outs=[wcol], ins=[tid])
    dtok = p.sb("mc_dtok", [128, 4 * DM], BF16)
    dfm = p.sb("mc_dfm", [128, NCH * 512], BF16)
    gwb = p.sb("mc_gwb", [128, NCH * 512], BF16)
    uwb = p.sb("mc_uwb", [128, NCH * 512], BF16)
    dwb = p.sb("mc_dwb", [128, 4 * DM], BF16)
    hidb = p.sb("mc_hid", [128, 4 * 512], BF16)
    ysb = p.sb("mc_ysb", [128, 1024], F32)
    tmp = p.SCR[0]
    for e in range(8):
        for c in range(NCH):
            p.dma(gwb, gwb[:, c * 512:(c + 1) * 512], gw_h,
                  gw_h.h.ap()[e, c * 128:(c + 1) * 128, :], eng="gpsimd")
            p.dma(uwb, uwb[:, c * 512:(c + 1) * 512], uw_h,
                  uw_h.h.ap()[e, c * 128:(c + 1) * 128, :], eng="gpsimd")
        for hc in range(4):
            p.dma(dwb, dwb[:, hc * DM:(hc + 1) * DM], dw_h,
                  dw_h.h.ap()[e, hc * 128:(hc + 1) * 128, :], eng="gpsimd")
        for q in range(4):
            p.memset(dtok, dtok[:, q * DM:(q + 1) * DM], 0.0)
            p.igather(dtok, dtok[:, q * DM:(q + 1) * DM], xall,
                      xall.ap(), tix, tix[:, e * 4 + q:e * 4 + q + 1],
                      32767)
        for q in range(4):
            for c in range(NCH):
                p.transp(p.PBbf, p.PBbf[:, 0:128], dtok,
                         dtok[:, q * DM + c * 128:q * DM + (c + 1) * 128],
                         C["iden"])
                p.copy(dfm, dfm[:, c * 512 + q * 128:c * 512 + (q + 1) * 128],
                       p.PBbf, p.PBbf[:, 0:128])
        for hc in range(4):
            g, u = p.PB[0], p.PB[1]
            for c in range(NCH):
                p.mm(g, g.ap(), gwb,
                     gwb[:, c * 512 + hc * 128:c * 512 + (hc + 1) * 128],
                     dfm, dfm[:, c * 512:(c + 1) * 512],
                     start=(c == 0), stop=(c == NCH - 1))
                p.mm(u, u.ap(), uwb,
                     uwb[:, c * 512 + hc * 128:c * 512 + (hc + 1) * 128],
                     dfm, dfm[:, c * 512:(c + 1) * 512],
                     start=(c == 0), stop=(c == NCH - 1))
            p.act(tmp, tmp.ap(), g, g.ap(), AF.Silu)
            p.tt(hidb, hidb[:, hc * 512:(hc + 1) * 512], tmp, tmp.ap(),
                 u, u.ap(), ALU.mult)
        for oc in range(NCH):
            pb = p.PB[oc % 2]
            for hc in range(4):
                p.mm(pb, pb.ap(), dwb,
                     dwb[:, hc * DM + oc * 128:hc * DM + (oc + 1) * 128],
                     hidb, hidb[:, hc * 512:(hc + 1) * 512],
                     start=(hc == 0), stop=(hc == 3))
            p.copy(dfm, dfm[:, oc * 512:(oc + 1) * 512], pb, pb.ap())
        for q in range(4):
            for c in range(NCH):
                p.transp(p.PBbf, p.PBbf[:, 0:128], dfm,
                         dfm[:, c * 512 + q * 128:c * 512 + (q + 1) * 128],
                         C["iden"])
                p.emit("vector", lambda E, c=c, q=q, e=e: E.tensor_scalar(
                    ysb.h.ap()[:, c * 128:(c + 1) * 128],
                    p.PBbf.h.ap()[0:128, 0:128],
                    wcol.h.ap()[:, e * 4 + q:e * 4 + q + 1], None,
                    ALU.mult), outs=[ysb], ins=[p.PBbf, wcol])
            p.dma(yout, yout.h.ap()[e * 512 + q * 128:
                                    e * 512 + (q + 1) * 128, :],
                  ysb, ysb.ap(), eng="gpsimd")
    p.finalize_timing([yout])
    return ["yout"]


def build_moe_gather(p, W, trange=None):
    yall = p.dram("yall", [32768, DM], F32, kind="ExternalInput")
    slotk_d = p.dram("slotk", [TPC, 8], F32, kind="ExternalInput")
    msum_d = p.dram("msum", [TPC, DM], F32, kind="ExternalOutput")
    gsum = p.sb("dg_gsum", [128, DM], F32)
    sk = p.sb("dg_sk", [128, 8], F32)
    ski = p.sb("dg_ski", [128, 8], I32)
    glist = range(TPC // 128) if trange is None else range(*trange)
    for gt in glist:
        p.memset(gsum, gsum.ap(), 0.0)
        p.dma(sk, sk.ap(), slotk_d,
              slotk_d.h.ap()[gt * 128:(gt + 1) * 128, :])
        p.ts(sk, sk.ap(), sk, sk.ap(), 1.0e5, ALU.min)
        p.copy(ski, ski.ap(), sk, sk.ap())
        for k in range(8):
            p.igather(gsum, gsum.ap(), yall, yall.ap(), ski,
                      ski[:, k:k + 1], 32767, compute_op=ALU.add)
        p.dma(msum_d, msum_d.h.ap()[gt * 128:(gt + 1) * 128, :], gsum,
              gsum.ap(), eng="gpsimd")
    p.finalize_timing([msum_d])
    return ["msum"]


def build_moe_combine(p, W):
    C = std_consts(p)
    xin = p.dram("xin", [SEQ_PER_CORE, DM, SEQ], F32, kind="ExternalInput")
    msum_d = p.dram("msum", [TPC, DM], F32, kind="ExternalInput")
    xout = p.dram("xout", [SEQ_PER_CORE, DM, SEQ], F32, kind="ExternalOutput")
    sg_h = p.const("md_sg", W["L0_ffn_shared_gate_w"].astype(np.float32))
    su_h = p.const("md_su", W["L0_ffn_shared_up_w"].astype(np.float32))
    sd_h = p.const("md_sd", W["L0_ffn_shared_down_w"].astype(np.float32))
    stream = p.sb("stream", [128, NCH * SEQ], F32)
    fout = p.sb("fout", [128, NCH * SEQ], BF16)
    sbf = p.sb("md_sbf", [128, NCH * SEQ], BF16)
    sgw = p.sb("md_sgw", [128, NCH * 512], BF16)
    suw = p.sb("md_suw", [128, NCH * 512], BF16)
    sdw = p.sb("md_sdw", [128, 4 * DM], BF16)
    hid = p.sb("md_hid", [128, 4 * SEQ], BF16)
    gsum = p.sb("md_gsum", [128, DM], F32)
    tmp = p.SCR[0]
    for c in range(NCH):
        p.dma(sgw, sgw[:, c * 512:(c + 1) * 512], sg_h,
              sg_h[c * 128:(c + 1) * 128, :], eng="gpsimd")
        p.dma(suw, suw[:, c * 512:(c + 1) * 512], su_h,
              su_h[c * 128:(c + 1) * 128, :], eng="gpsimd")
    for hc in range(4):
        p.dma(sdw, sdw[:, hc * DM:(hc + 1) * DM], sd_h,
              sd_h[hc * 128:(hc + 1) * 128, :], eng="gpsimd")
    for s in range(SEQ_PER_CORE):
        for c in range(NCH):
            p.dma(stream, stream[:, c * SEQ:(c + 1) * SEQ],
                  xin, xin[s, c * 128:(c + 1) * 128, :])
            p.copy(sbf, sbf[:, c * SEQ:(c + 1) * SEQ], stream,
                   stream[:, c * SEQ:(c + 1) * SEQ])
        for hc in range(4):
            for tq in range(SEQ // 512):
                g, u = p.PB[0], p.PB[1]
                for c in range(NCH):
                    rhs = sbf[:, c * SEQ + tq * 512:c * SEQ + tq * 512 + 512]
                    p.mm(g, g.ap(), sgw,
                         sgw[:, c * 512 + hc * 128:c * 512 + (hc + 1) * 128],
                         sbf, rhs, start=(c == 0), stop=(c == NCH - 1))
                    p.mm(u, u.ap(), suw,
                         suw[:, c * 512 + hc * 128:c * 512 + (hc + 1) * 128],
                         sbf, rhs, start=(c == 0), stop=(c == NCH - 1))
                p.act(tmp, tmp.ap(), g, g.ap(), AF.Silu)
                p.tt(hid, hid[:, hc * SEQ + tq * 512:hc * SEQ + tq * 512 + 512],
                     tmp, tmp.ap(), u, u.ap(), ALU.mult)
        for oc in range(NCH):
            for tq in range(SEQ // 512):
                pb = p.PB[oc % 2]
                for hc in range(4):
                    p.mm(pb, pb.ap(), sdw,
                         sdw[:, hc * DM + oc * 128:hc * DM + (oc + 1) * 128],
                         hid, hid[:, hc * SEQ + tq * 512:hc * SEQ + tq * 512 + 512],
                         start=(hc == 0), stop=(hc == 3))
                p.copy(fout, fout[:, oc * SEQ + tq * 512:oc * SEQ + tq * 512 + 512],
                       pb, pb.ap())
        for tt in range(SEQ // 128):
            gt = s * (SEQ // 128) + tt
            p.dma(gsum, gsum.ap(), msum_d,
                  msum_d.h.ap()[gt * 128:(gt + 1) * 128, :], eng="sync")
            gsb = p.sb("md_gsb", [128, DM], BF16)
            p.copy(gsb, gsb.ap(), gsum, gsum.ap())
            for c in range(NCH):
                p.transp(p.PBbf, p.PBbf[:, 0:128], gsb,
                         gsb[:, c * 128:(c + 1) * 128], C["iden"])
                p.emit("vector", lambda E, c=c, tt=tt: E.tensor_tensor(
                    fout.h.ap()[:, c * SEQ + tt * 128:c * SEQ + (tt + 1) * 128],
                    fout.h.ap()[:, c * SEQ + tt * 128:c * SEQ + (tt + 1) * 128],
                    p.PBbf.h.ap()[:, 0:128], ALU.add),
                    outs=[fout], ins=[fout, p.PBbf])
        emit_postln(p, stream, fout, C, eps=1e-5)
        for c in range(NCH):
            p.dma(xout, xout[s, c * 128:(c + 1) * 128, :],
                  stream, stream[:, c * SEQ:(c + 1) * SEQ], eng="gpsimd")
    p.finalize_timing([xout])
    return ["xout"]



def run_launch(build_fn, per_core, weights, trace=False, scratch=16384):
    p = P(scratch=scratch)
    outs = build_fn(p, weights)
    nc = p.build()
    in_maps = [dict(pc, **p.const_inputs) for pc in per_core]
    res = run_bass_kernel_spmd(nc, in_maps, core_ids=list(range(NCORE)),
                               trace=trace)
    return [[r[o] for o in outs] for r in res.results], res


def build_block_launch(body_fn, weights, nseq=SEQ_PER_CORE, eps=1e-5):
    def build(p, W):
        C = std_consts(p)
        xin = p.dram("xin", [nseq, DM, SEQ], F32, kind="ExternalInput")
        xout = p.dram("xout", [nseq, DM, SEQ], F32, kind="ExternalOutput")
        stream = p.sb("stream", [128, NCH * SEQ], F32)
        fout = p.sb("fout", [128, NCH * SEQ], BF16)
        for s in range(nseq):
            for c in range(NCH):
                p.dma(stream, stream[:, c * SEQ:(c + 1) * SEQ],
                      xin, xin[s, c * 128:(c + 1) * 128, :])
            body_fn(p, stream, fout, W, C)
            emit_postln(p, stream, fout, C, eps=eps)
            for c in range(NCH):
                p.dma(xout, xout[s, c * 128:(c + 1) * 128, :],
                      stream, stream[:, c * SEQ:(c + 1) * SEQ], eng="gpsimd")
        p.finalize_timing([xout])
        return ["xout"]
    return build


def shard_stream(x_full):
    xt = np.ascontiguousarray(x_full.transpose(0, 2, 1).astype(np.float32))
    return [xt[c * SEQ_PER_CORE:(c + 1) * SEQ_PER_CORE] for c in range(NCORE)]


def unshard_stream(shards):
    x = np.concatenate(shards, axis=0)
    return np.ascontiguousarray(x.transpose(0, 2, 1))


def run_moe_chain(shards, W):
    import functools
    outsA, _ = run_launch(build_moe_route, [{"xin": s} for s in shards], W)
    counts_all = np.ascontiguousarray(
        np.concatenate([o[0] for o in outsA], axis=1).T).astype(np.float32)
    pcB = []
    for c in range(NCORE):
        mask = np.zeros((8, 1), np.float32)
        mask[:c] = 1.0
        tokid = (np.arange(TPC, dtype=np.float32) + c * TPC).reshape(-1, 128)
        pcB.append({"counts_all": counts_all, "coremask": mask,
                    "tokid": np.ascontiguousarray(tokid),
                    "excl": outsA[c][2], "mexp": outsA[c][3],
                    "w8": outsA[c][4], "e8": outsA[c][5]})
    tw_pieces, slotk_pieces = [], []
    for lo in range(0, 32, 4):
        bfn = functools.partial(build_moe_slots, trange=(lo, lo + 4))
        outsB, _ = run_launch(bfn, pcB, W)
        tw_pieces.append([o[0] for o in outsB])
        slotk_pieces.append([o[1][lo * 128:(lo + 4) * 128] for o in outsB])
    slotk = [np.concatenate([sp[c] for sp in slotk_pieces], axis=0)
             for c in range(NCORE)]
    xall = np.concatenate([o[6] for o in outsA], axis=0)
    pcC = []
    for c in range(NCORE):
        tw64 = np.stack([tw_pieces[pi][r]
                         .reshape(32768, 2)[c * TPC:(c + 1) * TPC]
                         for pi in range(8) for r in range(8)], axis=0)
        pcC.append({"tw8": np.ascontiguousarray(tw64), "xall": xall,
                    "gw8": np.ascontiguousarray(
                        W["L0_ffn_gate_w"][c * 8:(c + 1) * 8]
                        .astype(np.float32)),
                    "uw8": np.ascontiguousarray(
                        W["L0_ffn_up_w"][c * 8:(c + 1) * 8]
                        .astype(np.float32)),
                    "dw8": np.ascontiguousarray(
                        W["L0_ffn_down_w"][c * 8:(c + 1) * 8]
                        .astype(np.float32))})
    outsC, _ = run_launch(build_moe_expert, pcC, W)
    yall = np.concatenate([o[0] for o in outsC], axis=0)
    msums = []
    for lo in range(0, 32, 4):
        dfn = functools.partial(build_moe_gather, trange=(lo, lo + 4))
        pcD1 = [{"yall": yall, "slotk": slotk[c]} for c in range(NCORE)]
        outsD1, _ = run_launch(dfn, pcD1, W)
        msums.append([o[0][lo * 128:(lo + 4) * 128] for o in outsD1])
    msum = [np.concatenate([mp[c] for mp in msums], axis=0)
            for c in range(NCORE)]
    pcD = [{"xin": shards[c], "msum": msum[c]} for c in range(NCORE)]
    outsD, _ = run_launch(build_moe_combine, pcD, W)
    return [o[0] for o in outsD]


def kernel(**inputs):
    W = inputs
    shards = shard_stream(inputs["L0_attn_x"])
    for body, eps in ((body_l0_attn, 1e-5),):
        build = build_block_launch(body, W)
        outs, _ = run_launch(build, [{"xin": s} for s in shards], W)
        shards = [o[0] for o in outs]
    shards = run_moe_chain(shards, W)
    for body in (body_l1_attn, body_l1_ffn, body_l2_attn, body_l2_ffn,
                 body_l3_attn, body_l3_ffn):
        build = build_block_launch(body, W)
        outs, _ = run_launch(build, [{"xin": s} for s in shards], W)
        shards = [o[0] for o in outs]
    return unshard_stream(shards).astype(np.float32)
